# version 01
# baseline (speedup 1.0000x reference)
import numpy as np
import ml_dtypes
import concourse.bass as bass
import concourse.mybir as mybir
from concourse.bass_utils import run_bass_kernel_spmd
from contextlib import ExitStack

F32 = mybir.dt.float32
BF16 = mybir.dt.bfloat16
AF = mybir.ActivationFunctionType
ALU = mybir.AluOpType
BF16NP = ml_dtypes.bfloat16

B, T, C = 4, 8192, 1024
NG = C // 128
DEPTH = 6
HEADS, DH = 16, 64
WIN = 256
DI = 2730
DI2 = 2 * DI
NPAIR = (DI + 127) // 128
LN_EPS = 1e-5
WKV_EPS = 1e-8
QK_SCALE = 8.0

FT = 4992
WARM = 1024
RT = FT + WARM
SPLIT = 4928
ODD_START = 3200

P0C = 512
RWC = 448
ATC = 896
FFC = 512
CMC = 512

NEG_MASK = -1.0e30

N_LAYERS = DEPTH
OUT_MODE = "full"
DEBUG_TAP = None
RUN_PHASES = ("p0", "rwkv", "layers", "combine")
SAME_ENGINE_SKIP = True


def _chunks(total, size):
    res, t = [], 0
    while t < total:
        res.append((t, min(size, total - t)))
        t += size
    return res


class Sem:
    __slots__ = ("name", "count", "h")

    def __init__(self, name):
        self.name, self.count, self.h = name, 0, None


class Res:
    __slots__ = ("name", "w", "readers")

    def __init__(self, name):
        self.name, self.w, self.readers = name, None, []


class OpRec:
    __slots__ = ("eng", "fn", "waits", "incs", "tag")

    def __init__(self, eng, fn, waits, incs, tag):
        self.eng, self.fn, self.waits, self.incs, self.tag = eng, fn, waits, incs, tag


ENGN = ("pe", "act", "dve", "pool", "sp")


class Prog:
    def __init__(self):
        self.ops = []
        self.esem = {e: Sem("e_" + e) for e in ENGN}
        self.qsem = {"sp": Sem("q_sp"), "pool": Sem("q_pool")}
        self.all_sems = list(self.esem.values()) + list(self.qsem.values())
        self.fence_list = []
        self.fence_pending = set()
        self.deferred = {e: [] for e in ENGN}

    def fence(self):
        for e in ENGN:
            if self.deferred[e]:
                self._force_inc(self.esem[e])
        self.fence_list = [[s, s.count] for s in self.all_sems if s.count > 0]
        self.fence_pending = set(ENGN) | {"q_sp", "q_pool"}

    def _force_inc(self, sem):
        eng = None
        for k, s in self.esem.items():
            if s is sem:
                eng = k
        assert eng is not None, "cannot force-inc a DMA queue sem"
        sem.count += 1
        self.ops.append(OpRec(eng, lambda e: e.nop(), [], [(sem, 1)], "forceinc"))
        val = sem.count
        for m in self.deferred[eng]:
            m[1] = val
        self.deferred[eng] = []

    def _mkwaits(self, eng, deps):
        best = {}
        own = self.esem.get(eng)
        for d in deps:
            if d is None:
                continue
            s, v = d[0], d[1]
            if s is own and SAME_ENGINE_SKIP:
                continue
            if v is None:
                self._force_inc(s)
                v = d[1]
                assert v is not None
            if v <= 0:
                continue
            k = id(s)
            if best.get(k, (None, -1))[1] < v:
                best[k] = (s, v)
        out = []
        for s, v in best.values():
            assert v <= s.count, f"wait on future {s.name}:{v}>{s.count}"
            out.append((s, v))
        return out

    FORCE_INC_ALL = True

    def op(self, eng, fn, reads=(), writes=(), inc=True, extra=(), tag=""):
        if self.FORCE_INC_ALL:
            inc = True
        sem = self.esem[eng]
        deps = list(extra)
        if eng in self.fence_pending:
            self.fence_pending.discard(eng)
            deps.extend(self.fence_list)
        for r in reads:
            deps.append(r.w)
        for r in writes:
            deps.append(r.w)
            deps.extend(r.readers)
        waits = self._mkwaits(eng, deps)
        if inc:
            sem.count += 1
            incs = [(sem, 1)]
            mark = [sem, sem.count]
            for m in self.deferred[eng]:
                m[1] = sem.count
            self.deferred[eng] = []
        else:
            incs = []
            mark = [sem, None]
            self.deferred[eng].append(mark)
        self.ops.append(OpRec(eng, fn, waits, incs, tag))
        for r in reads:
            r.readers.append(mark)
        for r in writes:
            r.w = mark
            r.readers = []
        return mark

    def dma(self, q, out_ap, in_ap, reads=(), writes=(), extra=(), tag="dma",
            slow=False):
        qs = self.qsem[q]
        deps = list(extra)
        fk = "q_" + q
        if fk in self.fence_pending:
            self.fence_pending.discard(fk)
            deps.extend(self.fence_list)
        for r in reads:
            deps.append(r.w)
        for r in writes:
            deps.append(r.w)
            deps.extend(r.readers)
        waits = self._mkwaits("#dma", deps)
        qs.count += 16
        self.ops.append(OpRec(
            q, lambda e, o=out_ap, i=in_ap, sl=slow: e.dma_start(
                out=o, in_=i, allow_slow_non_contiguous=sl),
            waits, [(qs, 16)], tag))
        mark = [qs, qs.count]
        for r in reads:
            r.readers.append(mark)
        for r in writes:
            r.w = mark
            r.readers = []
        return mark

    def raw_wait(self, eng, waits, tag=""):
        self.ops.append(OpRec(eng, None, list(waits), [], tag))

    def replay(self, eng_name, eng):
        for o in self.ops:
            if o.eng != eng_name:
                continue
            for s, v in o.waits:
                eng.wait_ge(s.h, v)
            instr = o.fn(eng) if o.fn is not None else None
            for s, n in o.incs:
                instr = instr.then_inc(s.h, n)


def _rotary_tables():
    n = T
    half = np.arange(0, DH, 2, dtype=np.float64)
    inv_freq = 1.0 / 10000.0 ** (half / DH)
    t = np.arange(n, dtype=np.float64)
    freqs = t[:, None] * inv_freq[None, :]
    base_scale = (half + 0.4 * DH) / (1.4 * DH)
    power = (t - n // 2) / (WIN // 2)
    scale = base_scale[None, :] ** power[:, None]
    cos, sin = np.cos(freqs), np.sin(freqs)

    def expand(a):
        return np.ascontiguousarray(np.concatenate([a, a], axis=1).T)

    def dup(a):
        return np.ascontiguousarray(np.concatenate([a, a], axis=0), dtype=np.float32)

    return (dup(expand(cos * scale)), dup(expand(sin * scale)),
            dup(expand(cos / scale)), dup(expand(sin / scale)))


def _mask_pattern(delta):
    r = np.arange(128)[:, None]
    c = np.arange(384)[None, :]
    d = delta + c - r
    return (((d < 0) | (d > WIN)).astype(np.float32) * NEG_MASK).astype(BF16NP)


def _perm_sigma():
    Pi = np.zeros((128, 128), dtype=np.float32)
    for blk in (0, 64):
        for m in range(0, 32):
            Pi[blk + m + 32, blk + m] = -1.0
        for m in range(32, 64):
            Pi[blk + m - 32, blk + m] = 1.0
    return Pi.astype(BF16NP)


def _blockdiag():
    p = np.zeros((128, 128), dtype=np.float32)
    p[0:64, 0:64] = 1.0
    p[64:128, 64:128] = 1.0
    return p.astype(BF16NP)


def host_prepare(inputs):
    x = np.asarray(inputs["L0_attn_x"])
    cs_q, ss_q, cs_k, ss_k = _rotary_tables()
    rs = lambda k: np.ascontiguousarray(np.asarray(inputs[k]).reshape(NG, 128))
    common = {
        "wk": np.ascontiguousarray(np.asarray(inputs["L0_attn_Wk"]).T),
        "wv": np.ascontiguousarray(np.asarray(inputs["L0_attn_Wv"]).T),
        "wr": np.ascontiguousarray(np.asarray(inputs["L0_attn_Wr"]).T),
        "wo": np.ascontiguousarray(np.asarray(inputs["L0_attn_Wo"]).T),
        "time_decay": rs("L0_attn_time_decay"),
        "tmk": rs("L0_attn_time_mix_k"),
        "tmv": rs("L0_attn_time_mix_v"),
        "tmr": rs("L0_attn_time_mix_r"),
        "ln_g": rs("L0_attn_ln_g"),
        "ln_b": rs("L0_attn_ln_b"),
        "f_wqkv": np.ascontiguousarray(inputs["L0_ffn_Wqkv"]),
        "f_wo": np.ascontiguousarray(inputs["L0_ffn_Wo"]),
        "f_w1": np.ascontiguousarray(inputs["L0_ffn_W1"]),
        "f_w2": np.ascontiguousarray(inputs["L0_ffn_W2"]),
        "f_qs": np.ascontiguousarray(inputs["L0_ffn_q_scale"]),
        "f_ks": np.ascontiguousarray(inputs["L0_ffn_k_scale"]),
        "f_lng": np.ascontiguousarray(np.asarray(inputs["L0_ffn_ln_g"]).reshape(DEPTH, NG, 128)),
        "f_lnb": np.ascontiguousarray(np.asarray(inputs["L0_ffn_ln_b"]).reshape(DEPTH, NG, 128)),
        "f_b1": np.ascontiguousarray(inputs["L0_ffn_b1"]),
        "f_b2": np.ascontiguousarray(np.asarray(inputs["L0_ffn_b2"]).reshape(DEPTH, NG, 128)),
        "pi_sigma": _perm_sigma(),
        "pat_blk": _blockdiag(),
        "pat_ones": np.full((128, 128), 1.0 / C, dtype=np.float32),
        "ident": np.eye(128, dtype=np.float32).astype(BF16NP),
        "mask0": _mask_pattern(0),
        "mask128": _mask_pattern(128),
        "mask256": _mask_pattern(256),
        "consts": np.tile(np.array([[LN_EPS, WKV_EPS, np.log(QK_SCALE), 1.0, 0.0]],
                                   dtype=np.float32), (128, 1)),
    }
    in_maps = []
    for core in range(8):
        b, h = core // 2, core % 2
        sf = 0 if h == 0 else ODD_START
        lo = sf - WARM
        xe = np.zeros((RT, C), dtype=np.float32)
        g0 = max(lo, 0)
        xe[g0 - lo:, :] = x[b, g0: sf + FT, :]
        mask = np.zeros((1, RT), dtype=np.float32)
        mask[0, g0 - lo:] = 1.0
        m = dict(common)
        m["x_ext"] = np.ascontiguousarray(xe.T)
        m["maskrow"] = mask
        m["cs_q"] = np.ascontiguousarray(cs_q[:, sf:sf + FT])
        m["ss_q"] = np.ascontiguousarray(ss_q[:, sf:sf + FT])
        m["cs_k"] = np.ascontiguousarray(cs_k[:, sf:sf + FT])
        m["ss_k"] = np.ascontiguousarray(ss_k[:, sf:sf + FT])
        in_maps.append(m)
    return in_maps


def host_assemble(results):
    out = np.empty((B, T, C), dtype=np.float32)
    for core in range(8):
        b, h = core // 2, core % 2
        o = np.asarray(results[core]["out"])
        if h == 0:
            out[b, 0:SPLIT] = o[:, 0:SPLIT].T
        else:
            out[b, SPLIT:T] = o[:, SPLIT - ODD_START:FT].T
    return out


class Ctx:
    pass


def cm_view(hbm_ap, t0, tlen):
    return hbm_ap[:, t0:t0 + tlen].rearrange("(g p) t -> p g t", p=128)


def sb3(view2d, tlen):
    return view2d.rearrange("p (g t) -> p g t", t=tlen)


A1_EL = 43680
A2_EL = 22528
A3_EL = 16896


class Arena:
    def __init__(self, nel, name):
        self.nel, self.name, self.cur = nel, name, 0

    def reset(self):
        self.cur = 0

    def take(self, n):
        o = self.cur
        self.cur += n
        assert self.cur <= self.nel, f"{self.name} overflow {self.cur}>{self.nel}"
        return o


def region(cx, key):
    r = cx.regions.get(key)
    if r is None:
        r = Res(str(key))
        cx.regions[key] = r
    return r


def a1v(cx, off, n):
    return cx.a1[:, off:off + n]


def a1v32(cx, off, n32):
    assert off % 2 == 0
    return cx.a1[:, off:off + 2 * n32].bitcast(F32)


def a2v(cx, off, n):
    return cx.a2[:, off:off + n]


def a2v32(cx, off, n32):
    assert off % 2 == 0
    return cx.a2[:, off:off + 2 * n32].bitcast(F32)


def a3v(cx, off, n):
    return cx.a3[:, off:off + n]


def a3vb(cx, off, n16):
    assert n16 % 2 == 0
    return cx.a3[:, off:off + n16 // 2].bitcast(BF16)


def declare_dram(cx):
    nc = cx.nc

    def din(name, shape, dt=F32):
        return nc.dram_tensor(name, list(shape), dt, kind="ExternalInput").ap()

    cx.x_ext = din("x_ext", (C, RT))
    cx.maskrow = din("maskrow", (1, RT))
    cx.wk = din("wk", (C, C)); cx.wv = din("wv", (C, C))
    cx.wr = din("wr", (C, C)); cx.wo = din("wo", (C, C))
    cx.time_decay = din("time_decay", (NG, 128))
    cx.tmk = din("tmk", (NG, 128)); cx.tmv = din("tmv", (NG, 128))
    cx.tmr = din("tmr", (NG, 128))
    cx.ln_g = din("ln_g", (NG, 128)); cx.ln_b = din("ln_b", (NG, 128))
    cx.f_wqkv = din("f_wqkv", (DEPTH, C, 3072))
    cx.f_wo = din("f_wo", (DEPTH, C, C))
    cx.f_w1 = din("f_w1", (DEPTH, C, DI2))
    cx.f_w2 = din("f_w2", (DEPTH, DI, C))
    cx.f_qs = din("f_qs", (DEPTH, DH)); cx.f_ks = din("f_ks", (DEPTH, DH))
    cx.f_lng = din("f_lng", (DEPTH, NG, 128))
    cx.f_lnb = din("f_lnb", (DEPTH, NG, 128))
    cx.f_b1 = din("f_b1", (DEPTH, DI2))
    cx.f_b2 = din("f_b2", (DEPTH, NG, 128))
    cx.pi_sigma = din("pi_sigma", (128, 128), BF16)
    cx.pat_blk = din("pat_blk", (128, 128), BF16)
    cx.pat_ones = din("pat_ones", (128, 128))
    cx.ident = din("ident", (128, 128), BF16)
    cx.masks_in = {0: din("mask0", (128, 384), BF16),
                   128: din("mask128", (128, 384), BF16),
                   256: din("mask256", (128, 384), BF16)}
    cx.cs_q = din("cs_q", (128, FT)); cx.ss_q = din("ss_q", (128, FT))
    cx.cs_k = din("cs_k", (128, FT)); cx.ss_k = din("ss_k", (128, FT))
    cx.consts = din("consts", (128, 5))

    cx.out_hbm = nc.dram_tensor("out", [C, FT], F32, kind="ExternalOutput").ap()
    cx.xn_hbm = nc.dram_tensor("xn_scr", [C, RT + 1], F32, kind="Internal").ap()
    cx.s_hbm = nc.dram_tensor("s_scr", [C, FT], F32, kind="Internal").ap()
    cx.sa_hbm = nc.dram_tensor("sa_scr", [C, FT + 1], F32, kind="Internal").ap()
    cx.sb_hbm = nc.dram_tensor("sb_scr", [C, FT + 1], F32, kind="Internal").ap()


def alloc_sbuf(cx, stack):
    nc = cx.nc
    e = stack.enter_context
    cx.a1 = e(nc.sbuf_tensor("a1", [128, A1_EL], BF16))
    cx.a2 = e(nc.sbuf_tensor("a2", [128, A2_EL], BF16))
    cx.a3 = e(nc.sbuf_tensor("a3", [128, A3_EL], F32))
    cx.pi_sb = e(nc.sbuf_tensor("pi_sb", [128, 128], BF16))
    cx.patb_sb = e(nc.sbuf_tensor("patb_sb", [128, 128], BF16))
    cx.pato_sb = e(nc.sbuf_tensor("pato_sb", [128, 128], F32))
    cx.ident_sb = e(nc.sbuf_tensor("ident_sb", [128, 128], BF16))
    cx.ones_sb = e(nc.sbuf_tensor("ones_sb", [128, 128], BF16))
    cx.zrow_sb = e(nc.sbuf_tensor("zrow_sb", [1, 512], BF16))
    cx.onesf_sb = e(nc.sbuf_tensor("onesf_sb", [1, 128], F32))
    cx.mask_sb = {d: e(nc.sbuf_tensor(f"mask{d}_sb", [128, 384], BF16))
                  for d in (0, 128, 256)}
    cx.consts_sb = e(nc.sbuf_tensor("consts_sb", [128, 5], F32))
    cx.cols = e(nc.sbuf_tensor("cols", [128, 160], F32))
    cx.psum = [e(nc.psum_tensor(f"ps{i}", [128, 512], F32)) for i in range(8)]
    for s in cx.prog.all_sems:
        s.h = e(nc.semaphore(s.name))
    cx.bank = [Res(f"bank{i}") for i in range(8)]
    cx.regions = {}
    cx.r_hbm = {n: Res("hbm_" + n) for n in ("xn", "s", "sa", "sb", "out")}
    cx.rc = Res("cols")
    cx.A1 = Arena(A1_EL, "a1")
    cx.A2 = Arena(A2_EL, "a2")
    cx.A3 = Arena(A3_EL, "a3")


def emit_setup(cx):
    p = cx.prog
    for ap_sb, src in ((cx.pi_sb, cx.pi_sigma), (cx.patb_sb, cx.pat_blk),
                       (cx.pato_sb, cx.pat_ones), (cx.ident_sb, cx.ident),
                       (cx.consts_sb, cx.consts)):
        p.dma("sp", ap_sb[:, :], src, writes=[region(cx, ("fix", ap_sb.name))],
              tag="ld_fix")
    for d in (0, 128, 256):
        p.dma("sp", cx.mask_sb[d][:, :], cx.masks_in[d],
              writes=[region(cx, ("fix", f"mask{d}"))], tag="ld_mask")
    p.op("dve", lambda e: e.memset(cx.ones_sb[:, :], 1.0),
         writes=[region(cx, ("fix", "ones"))], tag="ones")
    p.op("dve", lambda e: e.memset(cx.zrow_sb[:, :], 0.0),
         writes=[region(cx, ("fix", "zrow"))], tag="zrow")
    p.op("dve", lambda e: e.memset(cx.onesf_sb[:, :], 1.0),
         writes=[region(cx, ("fix", "onesf"))], tag="onesf")
    for j in range(NG):
        for base, src in ((0, cx.ln_g), (8, cx.ln_b), (16, cx.tmk),
                          (24, cx.tmv), (32, cx.tmr), (40, cx.time_decay)):
            p.dma("sp", cx.cols[:, base + j:base + j + 1],
                  src[j:j + 1, :].rearrange("a p -> p a"),
                  writes=[cx.rc], tag="ld_col")
    p.op("act", lambda e: e.activation(cx.cols[:, 40:48], cx.cols[:, 40:48], AF.Exp),
         reads=[cx.rc], writes=[cx.rc], tag="td_exp")
    zr = region(cx, ("a3", "zrow"))
    p.op("dve", lambda e: e.memset(a3v(cx, 0, NG), 0.0), writes=[zr], tag="zc")
    for name, hbm in (("xn", cx.xn_hbm), ("sa", cx.sa_hbm), ("sb", cx.sb_hbm)):
        for g in range(NG):
            p.dma("sp", hbm[128 * g:128 * (g + 1), 0:1], cx.a3[:, g:g + 1],
                  reads=[zr], writes=[cx.r_hbm[name]], tag="zg_" + name,
                  slow=True)


def emit_ln(cx, tag, L, r_src, src_off, writer, bm_i, bv_i, scr_off):
    p = cx.prog
    bm, bv = cx.psum[bm_i], cx.psum[bv_i]
    rbm, rbv = cx.bank[bm_i], cx.bank[bv_i]
    m2 = a3v(cx, scr_off, L)
    varb = a3v(cx, scr_off + L, L)
    rstdb = a3v(cx, scr_off + 2 * L, L)
    sq = a3v(cx, scr_off + 3 * L, L)
    t2 = a3v(cx, scr_off + 4 * L, L)
    r_m2 = region(cx, ("a3", scr_off))
    r_varb = region(cx, ("a3", scr_off + L))
    r_rstd = region(cx, ("a3", scr_off + 2 * L))
    r_sq = region(cx, ("a3", scr_off + 3 * L))
    r_t2 = region(cx, ("a3", scr_off + 4 * L))
    eps = cx.consts_sb[:, 0:1]
    for g in range(NG):
        xg = a3v(cx, src_off + g * L, L)
        p.op("act", lambda e, o=sq, i=xg: e.activation(o, i, AF.Square),
             reads=[r_src], writes=[r_sq], tag=tag + "_sq")
        for c0 in range(0, L, 512):
            w = min(512, L - c0)
            st, sp_ = (g == 0), (g == NG - 1)
            p.op("pe", lambda e, o=bm[:, c0:c0 + w], r=xg[:, c0:c0 + w],
                 st=st, sp=sp_: e.matmul(o, cx.pato_sb[:, :], r, start=st, stop=sp),
                 reads=[r_src], writes=[rbm], inc=sp_, tag=tag + "_mm")
            p.op("pe", lambda e, o=bv[:, c0:c0 + w], r=sq[:, c0:c0 + w],
                 st=st, sp=sp_: e.matmul(o, cx.pato_sb[:, :], r, start=st, stop=sp),
                 reads=[r_sq], writes=[rbv], inc=sp_, tag=tag + "_mv")
    p.op("act", lambda e: e.activation(m2, bm[:, 0:L], AF.Square),
         reads=[rbm], writes=[r_m2], tag=tag + "_m2")
    p.op("dve", lambda e: e.tensor_tensor(varb, bv[:, 0:L], m2, ALU.subtract),
         reads=[rbv, r_m2], writes=[r_varb], tag=tag + "_var")
    p.op("act", lambda e: e.activation(rstdb, varb, AF.Ln, bias=eps),
         reads=[r_varb], writes=[r_rstd], tag=tag + "_lnv")
    p.op("act", lambda e: e.activation(rstdb, rstdb, AF.Exp, scale=-0.5),
         reads=[r_rstd], writes=[r_rstd], tag=tag + "_expv")
    for g in range(NG):
        xg = a3v(cx, src_off + g * L, L)
        p.op("dve", lambda e, i=xg: e.tensor_tensor(sq, i, bm[:, 0:L], ALU.subtract),
             reads=[r_src, rbm], writes=[r_sq], tag=tag + "_sub")
        p.op("dve", lambda e: e.tensor_tensor(t2, sq, rstdb, ALU.mult),
             reads=[r_sq, r_rstd], writes=[r_t2], tag=tag + "_mul")
        writer(g, t2, r_t2)


def emit_p0(cx):
    p = cx.prog
    A = cx.A3
    A.reset()
    XA = A.take(NG * P0C)
    OUT = A.take(NG * P0C)
    SCR = A.take(5 * P0C)
    r_xa = region(cx, ("a3", "p0xa"))
    r_out = region(cx, ("a3", "p0out"))
    for (t0, L) in _chunks(RT, P0C):
        p.dma("sp", sb3(a3v(cx, XA, NG * L), L), cm_view(cx.x_ext, t0, L),
              writes=[r_xa], tag=f"p0_ld@{t0}")

        def writer(g, t2, r_t2, L=L):
            p.op("dve", lambda e, g=g, t2=t2: e.tensor_copy(a3v(cx, OUT + g * L, L), t2),
                 reads=[r_t2], writes=[r_out], tag="p0_o")

        emit_ln(cx, f"p0@{t0}", L, r_xa, XA, writer, 6, 7, SCR)
        p.dma("sp", cm_view(cx.xn_hbm, 1 + t0, L), sb3(a3v(cx, OUT, NG * L), L),
              reads=[r_out], writes=[cx.r_hbm["xn"]], tag=f"p0_st@{t0}")


def emit_rwkv(cx):
    p = cx.prog
    A1, A2, A3 = cx.A1, cx.A2, cx.A3
    A1.reset(); A2.reset(); A3.reset()

    woff = {}
    wres = {}
    for name, hbm in (("wk", cx.wk), ("wv", cx.wv), ("wr", cx.wr), ("wo", cx.wo)):
        off = A1.take(NG * C)
        woff[name] = off
        rw = region(cx, ("a1", off))
        wres[name] = rw
        for g in range(NG):
            p.dma("pool", a1v(cx, off + g * C, C), hbm[128 * g:128 * (g + 1), :],
                  writes=[rw], tag=f"rw_ldw_{name}")
    SS0 = A1.take(2 * RWC)
    SS1 = A1.take(2 * RWC)
    EKV = A1.take(2 * NG * RWC)
    r_ss = [region(cx, ("a1", SS0)), region(cx, ("a1", SS1))]
    r_ekv = region(cx, ("a1", EKV))

    VO = A2.take(2 * NG * RWC)
    SRO = A2.take(2 * NG * RWC)
    DO = A2.take(2 * NG * RWC)
    r_v = region(cx, ("a2", VO))
    r_sr = region(cx, ("a2", SRO))
    r_den = region(cx, ("a2", DO))
    vv = a2v32(cx, VO, NG * RWC)
    srv = a2v32(cx, SRO, NG * RWC)
    denv = a2v32(cx, DO, NG * RWC)
    ekvv = a1v32(cx, EKV, NG * RWC)

    XN = A3.take(NG * (RWC + 1))
    XN2 = A3.take(NG * (RWC + 1))
    EK = A3.take(NG * RWC)
    SCR = A3.take(5 * (RWC + 1))
    XMB = A3.take((NG * RWC + 1) // 2)
    MRB = A3.take(RWC + 1)
    r_xn = region(cx, ("a3", XN))
    r_xn2 = region(cx, ("a3", XN2))
    r_ek = region(cx, ("a3", EK))
    r_xm = region(cx, ("a3", XMB))
    r_mb = region(cx, ("a3", MRB))

    cden = cx.cols[:, 76:84]
    cnum = cx.cols[:, 84:92]
    r_carry = region(cx, ("cols", 76))
    p.op("dve", lambda e: e.memset(cx.cols[:, 76:92], 0.0), writes=[r_carry],
         tag="rw_c0")

    col = lambda j: cx.cols[:, j:j + 1]

    for (u0, L) in _chunks(RT, RWC):
        p.dma("sp", sb3(a3v(cx, XN, NG * (L + 1)), L + 1),
              cm_view(cx.xn_hbm, u0, L + 1),
              writes=[r_xn], tag=f"rw_ldxn@{u0}")

        L1 = L + 1

        def wr_xn2(g, t2, r_t2, L1=L1):
            p.op("dve", lambda e, g=g, t2=t2: e.tensor_scalar(
                a3v(cx, XN2 + g * L1, L1), t2, col(0 + g), col(8 + g),
                ALU.mult, ALU.add),
                reads=[r_t2, cx.rc], writes=[r_xn2], tag="rw_xn2")

        emit_ln(cx, f"rwln@{u0}", L1, r_xn, XN, wr_xn2, 6, 7, SCR)

        need_mask = u0 < WARM
        if need_mask:
            bmk, rbmk = cx.psum[5], cx.bank[5]
            mrow = a3vb(cx, SCR, 512)
            r_mrow = region(cx, ("a3", SCR))
            m0 = max(u0 - 1, 0)
            mlen = u0 + L - m0
            moff = 1 if u0 == 0 else 0
            p.dma("pool", mrow[0:1, 0:mlen], cx.maskrow[:, m0:m0 + mlen],
                  writes=[r_mrow], tag="rw_ldmrow")
            for c0 in range(0, mlen, 512):
                w = min(512, mlen - c0)
                p.op("pe", lambda e, o=bmk[:, c0:c0 + w],
                     r=mrow[0:1, c0:c0 + w]:
                     e.matmul(o, cx.ones_sb[0:1, :], r, start=True, stop=True),
                     reads=[r_mrow], writes=[rbmk],
                     inc=(c0 + 512 >= mlen), tag="rw_mb_mm")
            p.op("dve", lambda e, o=a3v(cx, MRB + moff, mlen), i=bmk[:, 0:mlen]:
                 e.tensor_copy(o, i),
                 reads=[rbmk], writes=[r_mb], tag="rw_mb")
            if u0 == 0:
                p.op("dve", lambda e: e.memset(a3v(cx, MRB, 1), 0.0),
                     writes=[r_mb], tag="rw_mb0")
            for g in range(NG):
                p.op("dve", lambda e, a=a3v(cx, XN2 + g * L1, L1),
                     m=a3v(cx, MRB, L1): e.tensor_tensor(a, a, m, ALU.mult),
                     reads=[r_xn2, r_mb], writes=[r_xn2], tag="rw_mxn2")

        xm = a3vb(cx, XMB, NG * L if (NG * L) % 2 == 0 else NG * L + 1)

        def mix(tm_base):
            for g in range(NG):
                cur = a3v(cx, XN2 + g * L1 + 1, L)
                prev = a3v(cx, XN2 + g * L1, L)
                dscr = a3v(cx, SCR + 4 * L1, L)
                r_d = region(cx, ("a3", SCR + 4 * L1))
                p.op("dve", lambda e, a=cur, b=prev, o=dscr:
                     e.tensor_tensor(o, a, b, ALU.subtract),
                     reads=[r_xn2], writes=[r_d], tag="rw_dm")
                p.op("dve", lambda e, o=xm[:, g * L:(g + 1) * L], i=dscr,
                     s=col(tm_base + g), b=prev:
                     e.scalar_tensor_tensor(o, i, s, b, ALU.mult, ALU.add),
                     reads=[r_d, r_xn2, cx.rc], writes=[r_xm], tag="rw_mix")

        def matmul_evac(wname, evac):
            off = woff[wname]
            rw = wres[wname]
            for og in range(NG):
                bk, rbk = cx.psum[og % 2], cx.bank[og % 2]
                for c0 in range(0, L, 512):
                    w = min(512, L - c0)
                    for ig in range(NG):
                        p.op("pe", lambda e, o=bk[:, c0:c0 + w],
                             l=a1v(cx, off + ig * C + og * 128, 128),
                             r=xm[:, ig * L + c0:ig * L + c0 + w],
                             st=(ig == 0), sp=(ig == NG - 1):
                             e.matmul(o, l, r, start=st, stop=sp),
                             reads=[r_xm, rw], writes=[rbk], inc=(ig == NG - 1),
                             tag=f"rw_mm_{wname}")
                    evac(og, c0, w, bk, rbk)

        def evac_ek(og, c0, w, bk, rbk):
            p.op("act", lambda e, o=a3v(cx, EK + og * L + c0, w), i=bk[:, c0:c0 + w]:
                 e.activation(o, i, AF.Exp), reads=[rbk], writes=[r_ek], tag="rw_ek")

        def evac_v(og, c0, w, bk, rbk):
            p.op("act", lambda e, o=vv[:, og * L + c0:og * L + c0 + w],
                 i=bk[:, c0:c0 + w]: e.activation(o, i, AF.Copy),
                 reads=[rbk], writes=[r_v], tag="rw_v")

        def evac_sr(og, c0, w, bk, rbk):
            p.op("act", lambda e, o=srv[:, og * L + c0:og * L + c0 + w],
                 i=bk[:, c0:c0 + w]: e.activation(o, i, AF.Sigmoid),
                 reads=[rbk], writes=[r_sr], tag="rw_sr")

        mix(16); matmul_evac("wk", evac_ek)
        mix(24); matmul_evac("wv", evac_v)
        mix(32); matmul_evac("wr", evac_sr)

        if need_mask:
            for g in range(NG):
                p.op("dve", lambda e, a=a3v(cx, EK + g * L, L),
                     m=a3v(cx, MRB + 1, L): e.tensor_tensor(a, a, m, ALU.mult),
                     reads=[r_ek, r_mb], writes=[r_ek], tag="rw_mek")
                p.op("dve", lambda e, a=vv[:, g * L:(g + 1) * L],
                     m=a3v(cx, MRB + 1, L): e.tensor_tensor(a, a, m, ALU.mult),
                     reads=[r_v, r_mb], writes=[r_v], tag="rw_mv")

        for g in range(NG):
            p.op("dve", lambda e, o=ekvv[:, g * L:(g + 1) * L],
                 a=a3v(cx, EK + g * L, L), b=vv[:, g * L:(g + 1) * L]:
                 e.tensor_tensor(o, a, b, ALU.mult),
                 reads=[r_ek, r_v], writes=[r_ekv], tag="rw_ekv")

        for g in range(NG):
            p.op("dve", lambda e, o=denv[:, g * L:(g + 1) * L],
                 tdb=col(40 + g).broadcast_to([128, L]),
                 d1=a3v(cx, EK + g * L, L), c0_=cden[:, g:g + 1]:
                 e.tensor_tensor_scan(o, tdb, d1, c0_, ALU.mult, ALU.add),
                 reads=[r_ek, r_carry, cx.rc], writes=[r_den], tag="rw_sd")
        for g in range(NG):
            p.op("dve", lambda e, o=a3v(cx, EK + g * L, L),
                 tdb=col(40 + g).broadcast_to([128, L]),
                 d1=ekvv[:, g * L:(g + 1) * L], c0_=cnum[:, g:g + 1]:
                 e.tensor_tensor_scan(o, tdb, d1, c0_, ALU.mult, ALU.add),
                 reads=[r_ekv, r_carry], writes=[r_ek], tag="rw_sn")
        for g in range(NG):
            p.op("dve", lambda e, o=cden[:, g:g + 1],
                 i=denv[:, g * L + L - 1:g * L + L]: e.tensor_copy(o, i),
                 reads=[r_den], writes=[r_carry], tag="rw_cd")
            p.op("dve", lambda e, o=cnum[:, g:g + 1],
                 i=a3v(cx, EK + g * L + L - 1, 1): e.tensor_copy(o, i),
                 reads=[r_ek], writes=[r_carry], tag="rw_cn")

        p.op("act", lambda e, n=NG * L: e.activation(denv[:, 0:n], denv[:, 0:n],
                                                     AF.Ln, bias=cx.consts_sb[:, 1:2]),
             reads=[r_den], writes=[r_den], tag="rw_lnd")
        p.op("act", lambda e, n=NG * L: e.activation(denv[:, 0:n], denv[:, 0:n],
                                                     AF.Exp, scale=-1.0),
             reads=[r_den], writes=[r_den], tag="rw_exd")
        sgt = xm
        for g in range(NG):
            p.op("dve", lambda e, o=vv[:, g * L:(g + 1) * L],
                 a=a3v(cx, EK + g * L, L), b=denv[:, g * L:(g + 1) * L]:
                 e.tensor_tensor(o, a, b, ALU.mult),
                 reads=[r_ek, r_den], writes=[r_v], tag="rw_wkv")
            p.op("dve", lambda e, o=sgt[:, g * L:(g + 1) * L],
                 a=vv[:, g * L:(g + 1) * L], b=srv[:, g * L:(g + 1) * L]:
                 e.tensor_tensor(o, a, b, ALU.mult),
                 reads=[r_v, r_sr], writes=[r_xm], tag="rw_sgt")

        if u0 + L > WARM:
            off = woff["wo"]
            rw = wres["wo"]
            lo = max(WARM, u0)
            o0 = lo - u0
            for og in range(NG):
                bk, rbk = cx.psum[2 + og % 2], cx.bank[2 + og % 2]
                for c0 in range(0, L, 512):
                    w = min(512, L - c0)
                    for ig in range(NG):
                        p.op("pe", lambda e, o=bk[:, c0:c0 + w],
                             l=a1v(cx, off + ig * C + og * 128, 128),
                             r=sgt[:, ig * L + c0:ig * L + c0 + w],
                             st=(ig == 0), sp=(ig == NG - 1):
                             e.matmul(o, l, r, start=st, stop=sp),
                             reads=[r_xm, rw], writes=[rbk], inc=(ig == NG - 1),
                             tag="rw_mm_s")
                sl = og % 2
                ssv = a1v32(cx, SS0 if sl == 0 else SS1, RWC)
                p.op("dve", lambda e, o=ssv[:, 0:L], i=bk[:, 0:L]: e.tensor_copy(o, i),
                     reads=[rbk], writes=[r_ss[sl]], tag="rw_sev")
                p.dma("sp",
                      cx.s_hbm[128 * og:128 * (og + 1),
                               u0 - WARM + o0:u0 - WARM + L],
                      ssv[:, o0:L], reads=[r_ss[sl]], writes=[cx.r_hbm["s"]],
                      tag="rw_sst")


def stream_pair(cx, layer):
    if layer == 0:
        return cx.xn_hbm, 1 + WARM, cx.sa_hbm
    return cx.sb_hbm, 1, cx.sa_hbm


def hbm_res_of(cx, hbm):
    if hbm is cx.xn_hbm:
        return cx.r_hbm["xn"]
    if hbm is cx.sa_hbm:
        return cx.r_hbm["sa"]
    if hbm is cx.sb_hbm:
        return cx.r_hbm["sb"]
    if hbm is cx.s_hbm:
        return cx.r_hbm["s"]
    return cx.r_hbm["out"]


def emit_attn(cx, layer):
    p = cx.prog
    src, sbase, dst = stream_pair(cx, layer)
    r_srcalifc = hbm_res_of(cx, src)
    r_dst = hbm_res_of(cx, dst)
    A1, A2, A3 = cx.A1, cx.A2, cx.A3
    A1.reset(); A2.reset(); A3.reset()

    XSB = A1.take(NG * ATC)
    QQ = A1.take(NG * ATC)
    KK = A1.take(NG * (256 + ATC))
    VT = A1.take((256 + ATC) // 128 * 1024)
    AO = A1.take(NG * ATC)
    r_xsb = region(cx, ("a1", XSB))
    r_qq = region(cx, ("a1", QQ))
    r_kk = region(cx, ("a1", KK))
    r_vt = region(cx, ("a1", VT))
    r_ao = region(cx, ("a1", AO))

    WOSB = A2.take(NG * C)
    WQ0 = A2.take(NG * 512)
    WQ1 = A2.take(NG * 512)
    EPT0 = A2.take(384)
    EPT1 = A2.take(384)
    r_wo = region(cx, ("a2", WOSB))
    r_wq = [region(cx, ("a2", WQ0)), region(cx, ("a2", WQ1))]
    r_ept = [region(cx, ("a2", EPT0)), region(cx, ("a2", EPT1))]

    TCQ = A3.take(ATC); TSQ = A3.take(ATC)
    TCK = A3.take(ATC); TSK = A3.take(ATC)
    QE0 = A3.take(256); QE1 = A3.take(256)
    SQ0 = A3.take(256); SQ1 = A3.take(256)
    T0B0 = A3.take(256); T0B1 = A3.take(256)
    T1_ = A3.take(512); T2_ = A3.take(512)
    RQB0 = A3.take(512); RQB1 = A3.take(512)
    XRES = A3.take(NG * 512)
    RECIP = A3.take(512)
    r_tab = region(cx, ("a3", "attab"))
    r_xres = region(cx, ("a3", XRES))
    r_recip = region(cx, ("a3", RECIP))

    r_qsrow = region(cx, ("a3", "qsrow"))
    qsrow = a3v(cx, T1_, DH)
    ksrow = a3v(cx, T1_ + DH, DH)
    mprod = a3v(cx, T1_ + 2 * DH, 1)
    for half in (0, 1):
        p.dma("sp", cx.cols[64 * half:64 * (half + 1), 72:73],
              cx.f_qs[layer:layer + 1, :].rearrange("a p -> p a"),
              writes=[cx.rc], tag="at_qscol")
        p.dma("sp", cx.cols[64 * half:64 * (half + 1), 73:74],
              cx.f_ks[layer:layer + 1, :].rearrange("a p -> p a"),
              writes=[cx.rc], tag="at_kscol")
    p.dma("sp", qsrow[0:1, :], cx.f_qs[layer:layer + 1, :], writes=[r_qsrow],
          tag="at_qsrow")
    p.dma("sp", ksrow[0:1, :], cx.f_ks[layer:layer + 1, :], writes=[r_qsrow],
          tag="at_ksrow")
    p.op("dve", lambda e: e.tensor_reduce(qsrow[0:1, 0:1], qsrow[0:1, :],
                                          mybir.AxisListType.X, ALU.max,
                                          apply_absolute_value=True),
         reads=[r_qsrow], writes=[r_qsrow], tag="at_qsmax")
    p.op("dve", lambda e: e.tensor_reduce(ksrow[0:1, 0:1], ksrow[0:1, :],
                                          mybir.AxisListType.X, ALU.max,
                                          apply_absolute_value=True),
         reads=[r_qsrow], writes=[r_qsrow], tag="at_ksmax")
    p.op("dve", lambda e: e.tensor_tensor(mprod[0:1, :], qsrow[0:1, 0:1],
                                          ksrow[0:1, 0:1], ALU.mult),
         reads=[r_qsrow], writes=[r_qsrow], tag="at_mprod")
    bn6 = cx.psum[6]
    p.op("pe", lambda e: e.matmul(bn6[:, 0:1], cx.onesf_sb[:, :], mprod[0:1, :],
                                  start=True, stop=True),
         reads=[r_qsrow, region(cx, ("fix", "onesf"))], writes=[cx.bank[6]],
         tag="at_negb_mm")
    p.op("act", lambda e: e.activation(cx.cols[:, 74:75], bn6[:, 0:1], AF.Copy,
                                       scale=-QK_SCALE),
         reads=[cx.bank[6]], writes=[cx.rc], tag="at_negb")

    for g in range(NG):
        p.dma("pool", a2v(cx, WOSB + g * C, C),
              cx.f_wo[layer, 128 * g:128 * (g + 1), :], writes=[r_wo],
              tag="at_ldwo")

    n_sl = lambda L: [(c0, min(512, L - c0)) for c0 in range(0, L, 512)]

    for ci, (t0, L) in enumerate(_chunks(FT, ATC)):
        for dst_off, srcap in ((TCQ, cx.cs_q), (TSQ, cx.ss_q),
                               (TCK, cx.cs_k), (TSK, cx.ss_k)):
            p.dma("sp", a3v(cx, dst_off, L), srcap[:, t0:t0 + L],
                  writes=[r_tab], tag="at_ldtab")
        p.dma("pool", sb3(a1v(cx, XSB, 4 * L), L),
              src[0:512, sbase + t0:sbase + t0 + L].rearrange(
                  "(g p) t -> p g t", p=128),
              writes=[r_xsb], tag="at_ldxs_a")
        p.dma("pool", sb3(a1v(cx, XSB + 4 * L, 4 * L), L),
              src[512:1024, sbase + t0 - 1:sbase + t0 - 1 + L].rearrange(
                  "(g p) t -> p g t", p=128),
              writes=[r_xsb], tag="at_ldxs_s")

        for ot in range(16):
            is_q = ot < 8
            g = ot % 8
            wq_off = (WQ0, WQ1)[ot % 2]
            rwq = r_wq[ot % 2]
            p.dma("pool", sb3(a2v(cx, wq_off, NG * 128), 128),
                  cx.f_wqkv[layer, :, 128 * ot:128 * (ot + 1)].rearrange(
                      "(g p) oc -> p g oc", p=128),
                  writes=[rwq], tag="at_ldwq")
            for c0, w in n_sl(L):
                bk_i = c0 // 512 % 2
                bk, rbk = cx.psum[bk_i], cx.bank[bk_i]
                for ig in range(NG):
                    p.op("pe", lambda e, o=bk[:, 0:w],
                         l=a2v(cx, wq_off + ig * 128, 128),
                         r=a1v(cx, XSB + ig * L + c0, w),
                         st=(ig == 0), sp=(ig == NG - 1):
                         e.matmul(o, l, r, start=st, stop=sp),
                         reads=[r_xsb, rwq], writes=[rbk], inc=(ig == NG - 1),
                         tag="at_mm_qk")
                par = c0 // 512 % 2
                qe = a3vb(cx, (QE0, QE1)[par], 512)[:, 0:w]
                sq = a3vb(cx, (SQ0, SQ1)[par], 512)[:, 0:w]
                t0b = a3vb(cx, (T0B0, T0B1)[par], 512)[:, 0:w]
                rqb = a3v(cx, (RQB0, RQB1)[par], w)
                t1 = a3v(cx, T1_, w)
                t2 = a3v(cx, T2_, w)
                r_qe = region(cx, ("a3", (QE0, QE1)[par]))
                r_sq = region(cx, ("a3", (SQ0, SQ1)[par]))
                r_t0b = region(cx, ("a3", (T0B0, T0B1)[par]))
                r_rqb = region(cx, ("a3", (RQB0, RQB1)[par]))
                r_t1 = region(cx, ("a3", T1_))
                r_t2 = region(cx, ("a3", T2_))
                p.op("act", lambda e, o=qe, i=bk[:, 0:w]: e.activation(o, i, AF.Copy),
                     reads=[rbk], writes=[r_qe], tag="at_evq")
                p.op("dve", lambda e, o=sq, i=qe: e.tensor_tensor(o, i, i, ALU.mult),
                     reads=[r_qe], writes=[r_sq], tag="at_sq")
                bs, rbs = cx.psum[4], cx.bank[4]
                p.op("pe", lambda e, o=bs[:, 0:w], r=sq:
                     e.matmul(o, cx.patb_sb[:, :], r, start=True, stop=True),
                     reads=[r_sq, region(cx, ("fix", "patb_sb"))], writes=[rbs],
                     tag="at_ssb")
                if is_q:
                    p.op("act", lambda e, o=rqb, i=bs[:, 0:w]:
                         e.activation(o, i, AF.Ln), reads=[rbs], writes=[r_rqb],
                         tag="at_lnq")
                    p.op("act", lambda e, o=rqb, i=rqb:
                         e.activation(o, i, AF.Exp, scale=-0.5,
                                      bias=cx.consts_sb[:, 2:3]),
                         reads=[r_rqb], writes=[r_rqb], tag="at_exq")
                else:
                    p.op("act", lambda e, o=rqb, i=bs[:, 0:w]:
                         e.activation(o, i, AF.Ln), reads=[rbs], writes=[r_rqb],
                         tag="at_lnk")
                    p.op("act", lambda e, o=rqb, i=rqb:
                         e.activation(o, i, AF.Exp, scale=-0.5),
                         reads=[r_rqb], writes=[r_rqb], tag="at_exk")
                scol = cx.cols[:, 72:73] if is_q else cx.cols[:, 73:74]
                p.op("dve", lambda e, o=t0b, i=qe, s=scol, b=rqb:
                     e.scalar_tensor_tensor(o, i, s, b, ALU.mult, ALU.mult),
                     reads=[r_qe, r_rqb, cx.rc], writes=[r_t0b], tag="at_t0")
                bp, rbp = cx.psum[5], cx.bank[5]
                p.op("pe", lambda e, o=bp[:, 0:w], r=t0b:
                     e.matmul(o, cx.pi_sb[:, :], r, start=True, stop=True),
                     reads=[r_t0b, region(cx, ("fix", "pi_sb"))], writes=[rbp],
                     tag="at_pi")
                cst = a3v(cx, (TCQ if is_q else TCK) + c0, w)
                sst = a3v(cx, (TSQ if is_q else TSK) + c0, w)
                p.op("dve", lambda e, o=t1, a=t0b, b=cst:
                     e.tensor_tensor(o, a, b, ALU.mult),
                     reads=[r_t0b, r_tab], writes=[r_t1], tag="at_t1")
                p.op("dve", lambda e, o=t2, a=bp[:, 0:w], b=sst:
                     e.tensor_tensor(o, a, b, ALU.mult),
                     reads=[rbp, r_tab], writes=[r_t2], tag="at_t2")
                dst_off = (QQ + g * L + c0) if is_q else (KK + g * (256 + ATC) + 256 + c0)
                dst_res = r_qq if is_q else r_kk
                p.op("dve", lambda e, o=a1v(cx, dst_off, w), a=t1, b=t2:
                     e.tensor_tensor(o, a, b, ALU.add),
                     reads=[r_t1, r_t2], writes=[dst_res], tag="at_qk")

        ntt = L // 128
        for h2 in range(2):
            wq_off = (WQ0, WQ1)[h2]
            rwq = r_wq[h2]
            p.dma("pool", sb3(a2v(cx, wq_off, NG * 512), 512),
                  cx.f_wqkv[layer, :, 2048 + 512 * h2:2048 + 512 * (h2 + 1)].rearrange(
                      "(g p) oc -> p g oc", p=128),
                  writes=[rwq], tag="at_ldwv")
            for tt in range(ntt):
                bk_i = 2 + tt % 2
                bk, rbk = cx.psum[bk_i], cx.bank[bk_i]
                for ig in range(NG):
                    p.op("pe", lambda e, o=bk[:, 0:512],
                         l=a1v(cx, XSB + ig * L + tt * 128, 128),
                         r=a2v(cx, wq_off + ig * 512, 512),
                         st=(ig == 0), sp=(ig == NG - 1):
                         e.matmul(o, l, r, start=st, stop=sp),
                         reads=[r_xsb, rwq], writes=[rbk], inc=(ig == NG - 1),
                         tag="at_mm_v")
                p.op("dve", lambda e, o=a1v(cx, VT + (2 + tt) * 1024 + 512 * h2, 512),
                     i=bk[:, 0:512]: e.tensor_copy(o, i),
                     reads=[rbk], writes=[r_vt], tag="at_evv")

        npieces = (L + 511) // 512
        for g in range(NG):
            j_start = max(t0 - 256, 0)
            kcs = list(range(j_start, t0 + L - 128 + 1, 128))
            last_touch = {}
            for ji, j in enumerate(kcs):
                a = max(j, t0)
                b_ = min(j + 384, t0 + L)
                if b_ <= a:
                    continue
                for h01 in range(2):
                    for pc in range(npieces):
                        lo = max(a - t0, pc * 512)
                        hi = min(b_ - t0, min(pc * 512 + 512, L))
                        if lo < hi:
                            last_touch[pc] = (ji, h01)
            for pc in range(npieces):
                w = min(512, L - pc * 512)
                for bi in (0 + pc % 2, 2 + pc % 2):
                    bz, rbz = cx.psum[bi], cx.bank[bi]
                    p.op("pe", lambda e, o=bz[:, 0:w], z=cx.zrow_sb[:, 0:w]:
                         e.matmul(o, cx.ones_sb[0:1, :], z,
                                  start=True, stop=False),
                         reads=[region(cx, ("fix", "zrow")),
                                region(cx, ("fix", "ones"))],
                         writes=[rbz], tag="at_zb")
            for ji, j in enumerate(kcs):
                kloc = j - t0 + 256
                a = max(j, t0)
                b_ = min(j + 384, t0 + L)
                width = b_ - a
                if width <= 0:
                    continue
                delta = a - j
                for h01 in range(2):
                    pr = slice(64 * h01, 64 * h01 + 64)
                    bs_i = 4 + h01
                    bs, rbs = cx.psum[bs_i], cx.bank[bs_i]
                    p.op("pe", lambda e, o=bs[:, 0:width],
                         l=a1v(cx, KK + g * (256 + ATC) + kloc, 128)[pr, :],
                         r=a1v(cx, QQ + g * L + (a - t0), width)[pr, :]:
                         e.matmul(o, l, r, start=True, stop=False),
                         reads=[r_kk, r_qq], writes=[rbs], inc=False, tag="at_sim")
                    p.op("pe", lambda e, o=bs[:, 0:width],
                         r=cx.mask_sb[delta][:, 0:width]:
                         e.matmul(o, cx.ident_sb[:, :], r, start=False, stop=True),
                         reads=[region(cx, ("fix", f"mask{delta}"))], writes=[rbs],
                         tag="at_msk")
                    ept = a2v(cx, (EPT0, EPT1)[h01], width)
                    p.op("act", lambda e, o=ept, i=bs[:, 0:width]:
                         e.activation(o, i, AF.Exp, bias=cx.cols[:, 74:75]),
                         reads=[rbs, cx.rc], writes=[r_ept[h01]], tag="at_exp")
                    for pc in range(npieces):
                        p0_, p1_ = pc * 512, min(pc * 512 + 512, L)
                        lo = max(a - t0, p0_)
                        hi = min(b_ - t0, p1_)
                        if lo >= hi:
                            continue
                        ew = slice(lo - (a - t0), hi - (a - t0))
                        br, rbr = cx.psum[0 + pc % 2], cx.bank[0 + pc % 2]
                        bo, rbo = cx.psum[2 + pc % 2], cx.bank[2 + pc % 2]
                        is_last = last_touch.get(pc) == (ji, h01)
                        p.op("pe", lambda e, o=br[64 * h01:64 * h01 + 64, lo - p0_:hi - p0_],
                             l=cx.ones_sb[:, 0:64], r=ept[:, ew], sp=is_last:
                             e.matmul(o, l, r, start=False, stop=sp),
                             reads=[r_ept[h01], region(cx, ("fix", "ones"))],
                             writes=[rbr], tag="at_rs")
                        p.op("pe", lambda e, o=bo[64 * h01:64 * h01 + 64, lo - p0_:hi - p0_],
                             l=a1v(cx, VT + (kloc // 128) * 1024 + (2 * g + h01) * 64, 64),
                             r=ept[:, ew], sp=is_last:
                             e.matmul(o, l, r, start=False, stop=sp),
                             reads=[r_ept[h01], r_vt], writes=[rbo], tag="at_pv")
            for pc in range(npieces):
                p0_, p1_ = pc * 512, min(pc * 512 + 512, L)
                w = p1_ - p0_
                br, rbr = cx.psum[0 + pc % 2], cx.bank[0 + pc % 2]
                bo, rbo = cx.psum[2 + pc % 2], cx.bank[2 + pc % 2]
                rec = a3v(cx, RECIP, w)
                p.op("act", lambda e, o=rec, i=br[:, 0:w]: e.activation(o, i, AF.Ln),
                     reads=[rbr], writes=[r_recip], tag="at_lnr")
                p.op("act", lambda e, o=rec, i=rec: e.activation(o, i, AF.Exp,
                                                                 scale=-1.0),
                     reads=[r_recip], writes=[r_recip], tag="at_exr")
                p.op("dve", lambda e, o=a1v(cx, AO + g * L + p0_, w), a=bo[:, 0:w],
                     b=rec: e.tensor_tensor(o, a, b, ALU.mult),
                     reads=[rbo, r_recip], writes=[r_ao], tag="at_ev_o")

        for c0, w in n_sl(L):
            p.dma("sp", sb3(a3v(cx, XRES, NG * w), w),
                  cm_view(src, sbase + t0 + c0, w), writes=[r_xres],
                  tag="at_ldres")
            for og in range(NG):
                bk, rbk = cx.psum[6 + og % 2], cx.bank[6 + og % 2]
                for ig in range(NG):
                    p.op("pe", lambda e, o=bk[:, 0:w],
                         l=a2v(cx, WOSB + ig * C + og * 128, 128),
                         r=a1v(cx, AO + ig * L + c0, w),
                         st=(ig == 0), sp=(ig == NG - 1):
                         e.matmul(o, l, r, start=st, stop=sp),
                         reads=[r_ao, r_wo], writes=[rbk], inc=(ig == NG - 1),
                         tag="at_mm_wo")
                p.op("dve", lambda e, og=og, o=a3v(cx, XRES + og * w, w),
                     a=bk[:, 0:w]: e.tensor_tensor(o, a, o, ALU.add),
                     reads=[rbk, r_xres], writes=[r_xres], tag="at_res")
            p.dma("sp", cm_view(dst, 1 + t0 + c0, w),
                  sb3(a3v(cx, XRES, NG * w), w),
                  reads=[r_xres], writes=[r_dst], tag="at_stres")

        if DEBUG_TAP in ("qq", "kk", "ao"):
            toff, tres = {"qq": (QQ, r_qq), "kk": (KK, r_kk),
                          "ao": (AO, r_ao)}[DEBUG_TAP]
            stride = (256 + ATC) if DEBUG_TAP == "kk" else L
            skip = 256 if DEBUG_TAP == "kk" else 0
            for c0 in range(0, L, 512):
                w = min(512, L - c0)
                r_tapb = region(cx, ("a3", XRES))
                for g in range(NG):
                    p.op("dve", lambda e, o=a3v(cx, XRES + g * w, w),
                         i=a1v(cx, toff + g * stride + skip + c0, w):
                         e.tensor_copy(o, i),
                         reads=[tres], writes=[r_tapb], tag="tapcp")
                p.dma("sp", cm_view(cx.out_hbm, t0 + c0, w),
                      sb3(a3v(cx, XRES, NG * w), w),
                      reads=[r_tapb], writes=[cx.r_hbm["out"]], tag="tapst")

        if t0 + L < FT:
            for g in range(NG):
                p.op("dve", lambda e, g=g, L=L: e.tensor_copy(
                    a1v(cx, KK + g * (256 + ATC), 256),
                    a1v(cx, KK + g * (256 + ATC) + L, 256)),
                    reads=[r_kk], writes=[r_kk], tag="at_ktail")
            p.op("dve", lambda e, L=L: e.tensor_copy(
                a1v(cx, VT, 2 * 1024), a1v(cx, VT + (L // 128) * 1024, 2 * 1024)),
                reads=[r_vt], writes=[r_vt], tag="at_vtail")


def emit_ffn(cx, layer):
    p = cx.prog
    src, sbase, dst = cx.sa_hbm, 1, cx.sb_hbm
    r_src = cx.r_hbm["sa"]
    r_dst = cx.r_hbm["sb"]
    A1, A2, A3 = cx.A1, cx.A2, cx.A3
    A1.reset(); A2.reset(); A3.reset()

    W1O = A1.take(NG * DI2)
    r_w1 = region(cx, ("a1", W1O))
    for g in range(NG):
        p.dma("pool", a1v(cx, W1O + g * DI2, DI2),
              cx.f_w1[layer, 128 * g:128 * (g + 1), :], writes=[r_w1],
              tag="ff_ldw1")
    W2O = A2.take(NPAIR * C)
    r_w2 = region(cx, ("a2", W2O))
    for icc in range(NPAIR):
        rows = min(128, DI - 128 * icc)
        p.dma("pool", cx.a2[0:rows, W2O + icc * C:W2O + (icc + 1) * C],
              cx.f_w2[layer, 128 * icc:128 * icc + rows, :], writes=[r_w2],
              tag="ff_ldw2")

    X2F = A3.take(NG * FFC)
    XNB = A3.take(NG * FFC // 2)
    XSB2 = A3.take(NG * FFC // 2)
    HB = A3.take(NPAIR * FFC // 2)
    SCR = A3.take(5 * FFC)
    GL = A3.take(FFC // 2)
    XOUT = XNB
    r_x2f = region(cx, ("a3", X2F))
    r_xnb = region(cx, ("a3", XNB))
    r_xsb2 = region(cx, ("a3", XSB2))
    r_h = region(cx, ("a3", HB))
    r_gl = region(cx, ("a3", GL))
    r_xout = r_xnb

    for j in range(NG):
        p.dma("sp", cx.cols[:, 48 + j:49 + j],
              cx.f_lng[layer, j:j + 1, :].rearrange("a p -> p a"),
              writes=[cx.rc], tag="ff_lng")
        p.dma("sp", cx.cols[:, 56 + j:57 + j],
              cx.f_lnb[layer, j:j + 1, :].rearrange("a p -> p a"),
              writes=[cx.rc], tag="ff_lnb")
        p.dma("sp", cx.cols[:, 64 + j:65 + j],
              cx.f_b2[layer, j:j + 1, :].rearrange("a p -> p a"),
              writes=[cx.rc], tag="ff_b2")
    for pr in range(NPAIR):
        wa = min(128, DI - 128 * pr)
        p.dma("sp", cx.cols[0:wa, 96 + pr:97 + pr],
              cx.f_b1[layer:layer + 1, 128 * pr:128 * pr + wa].rearrange("a p -> p a"),
              writes=[cx.rc], tag="ff_b1a")
        p.dma("sp", cx.cols[0:wa, 118 + pr:119 + pr],
              cx.f_b1[layer:layer + 1, DI + 128 * pr:DI + 128 * pr + wa].rearrange(
                  "a p -> p a"),
              writes=[cx.rc], tag="ff_b1g")

    col = lambda j: cx.cols[:, j:j + 1]

    for (t0, L) in _chunks(FT, FFC):
        p.dma("sp", sb3(a3v(cx, X2F, 4 * L), L),
              src[0:512, sbase + t0:sbase + t0 + L].rearrange(
                  "(g p) t -> p g t", p=128),
              writes=[r_x2f], tag="ff_ldx_a")
        p.dma("sp", sb3(a3v(cx, X2F + 4 * L, 4 * L), L),
              src[512:1024, sbase + t0 - 1:sbase + t0 - 1 + L].rearrange(
                  "(g p) t -> p g t", p=128),
              writes=[r_x2f], tag="ff_ldx_s")
        xsb2 = a3vb(cx, XSB2, NG * L)
        p.dma("pool", sb3(xsb2[:, 0:4 * L], L),
              src[0:512, sbase + t0:sbase + t0 + L].rearrange(
                  "(g p) t -> p g t", p=128),
              writes=[r_xsb2], tag="ff_ldxb_a")
        p.dma("pool", sb3(xsb2[:, 4 * L:8 * L], L),
              src[512:1024, sbase + t0 - 1:sbase + t0 - 1 + L].rearrange(
                  "(g p) t -> p g t", p=128),
              writes=[r_xsb2], tag="ff_ldxb_s")

        xnb = a3vb(cx, XNB, NG * L)

        def wr_xnb(g, t2, r_t2, xnb=xnb, L=L):
            p.op("dve", lambda e, o=xnb[:, g * L:(g + 1) * L], t2=t2,
                 s1=col(48 + g), s2=col(56 + g):
                 e.tensor_scalar(o, t2, s1, s2, ALU.mult, ALU.add),
                 reads=[r_t2, cx.rc], writes=[r_xnb], tag="ff_xnb")

        emit_ln(cx, f"ffln{layer}@{t0}", L, r_x2f, X2F, wr_xnb, 6, 7, SCR)

        hb = a3vb(cx, HB, NPAIR * FFC)
        for pr in range(NPAIR):
            wa = min(128, DI - 128 * pr)
            ba, rba = cx.psum[0 + pr % 2], cx.bank[0 + pr % 2]
            bg, rbg = cx.psum[2 + pr % 2], cx.bank[2 + pr % 2]
            for ig in range(NG):
                p.op("pe", lambda e, o=ba[0:wa, 0:L],
                     l=a1v(cx, W1O + ig * DI2 + 128 * pr, wa),
                     r=xnb[:, ig * L:ig * L + L],
                     st=(ig == 0), sp=(ig == NG - 1):
                     e.matmul(o, l, r, start=st, stop=sp),
                     reads=[r_xnb, r_w1], writes=[rba], inc=(ig == NG - 1),
                     tag="ff_mma")
            for ig in range(NG):
                p.op("pe", lambda e, o=bg[0:wa, 0:L],
                     l=a1v(cx, W1O + ig * DI2 + DI + 128 * pr, wa),
                     r=xnb[:, ig * L:ig * L + L],
                     st=(ig == 0), sp=(ig == NG - 1):
                     e.matmul(o, l, r, start=st, stop=sp),
                     reads=[r_xnb, r_w1], writes=[rbg], inc=(ig == NG - 1),
                     tag="ff_mmg")
            gl = a3vb(cx, GL, FFC)[0:wa, 0:L]
            p.op("act", lambda e, o=gl, i=bg[0:wa, 0:L],
                 bb=cx.cols[0:wa, 118 + pr:119 + pr]:
                 e.activation(o, i, AF.Gelu, bias=bb),
                 reads=[rbg, cx.rc], writes=[r_gl], tag="ff_gelu")
            p.op("dve", lambda e, o=hb[0:wa, pr * L:pr * L + L], i=ba[0:wa, 0:L],
                 s=cx.cols[0:wa, 96 + pr:97 + pr], b=gl:
                 e.scalar_tensor_tensor(o, i, s, b, ALU.add, ALU.mult),
                 reads=[rba, r_gl, cx.rc], writes=[r_h], tag="ff_h")

        p.dma("sp", sb3(a3v(cx, X2F, NG * L), L), cm_view(src, sbase + t0, L),
              writes=[r_x2f], tag="ff_ldxh")

        for og in range(NG):
            bk, rbk = cx.psum[4 + og % 2], cx.bank[4 + og % 2]
            for icc in range(NPAIR):
                wa = min(128, DI - 128 * icc)
                p.op("pe", lambda e, o=bk[:, 0:L],
                     l=cx.a2[0:wa, W2O + icc * C + 128 * og:W2O + icc * C + 128 * (og + 1)],
                     r=hb[0:wa, icc * L:icc * L + L],
                     st=(icc == 0), sp=(icc == NPAIR - 1):
                     e.matmul(o, l, r, start=st, stop=sp),
                     reads=[r_h, r_w2], writes=[rbk], inc=(icc == NPAIR - 1),
                     tag="ff_mm2")
            p.op("dve", lambda e, og=og, o=a3v(cx, XOUT + og * L, L),
                 i=bk[:, 0:L], s=col(64 + og), b=a3v(cx, X2F + og * L, L):
                 e.scalar_tensor_tensor(o, i, s, b, ALU.add, ALU.add),
                 reads=[rbk, r_x2f, cx.rc], writes=[r_xnb, r_xsb2],
                 tag="ff_res")
        p.dma("sp", cm_view(dst, 1 + t0, L), sb3(a3v(cx, XOUT, NG * L), L),
              reads=[r_xout], writes=[r_dst], tag="ff_st")


def emit_combine(cx):
    if OUT_MODE == "tap":
        return
    p = cx.prog
    A = cx.A3
    A.reset()
    ST = A.take(NG * CMC)
    SS = A.take(NG * CMC)
    XX = A.take(NG * CMC)
    GG = A.take(NG * CMC)
    r_st = region(cx, ("a3", "cmst"))
    r_ss = region(cx, ("a3", "cmss"))
    r_xx = region(cx, ("a3", "cmxx"))
    r_gg = region(cx, ("a3", "cmgg"))
    stream = cx.sb_hbm if N_LAYERS > 0 else cx.xn_hbm
    stream_base = 1 if N_LAYERS > 0 else 1 + WARM
    r_stream = hbm_res_of(cx, stream)
    for (t0, L) in _chunks(FT, CMC):
        if OUT_MODE == "xn":
            p.dma("sp", sb3(a3v(cx, GG, NG * L), L), cm_view(cx.xn_hbm, 1 + WARM + t0, L),
                  extra=[cx.r_hbm["xn"].w], writes=[r_gg], tag="cm_ldxn")
        elif OUT_MODE == "s":
            p.dma("sp", sb3(a3v(cx, GG, NG * L), L), cm_view(cx.s_hbm, t0, L),
                  extra=[cx.r_hbm["s"].w], writes=[r_gg], tag="cm_lds")
        elif OUT_MODE == "stream":
            p.dma("sp", sb3(a3v(cx, GG, NG * L), L), cm_view(stream, stream_base + t0, L),
                  extra=[r_stream.w], writes=[r_gg], tag="cm_ldst")
        else:
            p.dma("sp", sb3(a3v(cx, ST, NG * L), L), cm_view(stream, stream_base + t0, L),
                  extra=[r_stream.w], writes=[r_st], tag="cm_ldst")
            p.dma("sp", sb3(a3v(cx, SS, NG * L), L), cm_view(cx.s_hbm, t0, L),
                  extra=[cx.r_hbm["s"].w], writes=[r_ss], tag="cm_lds")
            p.dma("sp", sb3(a3v(cx, XX, NG * L), L), cm_view(cx.x_ext, WARM + t0, L),
                  writes=[r_xx], tag="cm_ldx")
            p.op("act", lambda e, L=L: e.activation(a3v(cx, GG, NG * L),
                                                    a3v(cx, ST, NG * L), AF.Silu),
                 reads=[r_st], writes=[r_gg], tag="cm_silu")
            p.op("dve", lambda e, L=L: e.tensor_tensor(
                a3v(cx, GG, NG * L), a3v(cx, GG, NG * L), a3v(cx, SS, NG * L),
                ALU.mult), reads=[r_gg, r_ss], writes=[r_gg], tag="cm_mul")
            p.op("dve", lambda e, L=L: e.tensor_tensor(
                a3v(cx, GG, NG * L), a3v(cx, GG, NG * L), a3v(cx, XX, NG * L),
                ALU.add), reads=[r_gg, r_xx], writes=[r_gg], tag="cm_add")
        p.dma("sp", cm_view(cx.out_hbm, t0, L), sb3(a3v(cx, GG, NG * L), L),
              reads=[r_gg], writes=[cx.r_hbm["out"]], tag="cm_st")


def emit_fence(cx):
    p = cx.prog
    p.raw_wait("sp", [(p.qsem["sp"], p.qsem["sp"].count)], tag="fence_sp")
    p.raw_wait("pool", [(p.qsem["pool"], p.qsem["pool"].count)], tag="fence_pool")


_CACHE = {}


def _build():
    key = (N_LAYERS, OUT_MODE, RUN_PHASES, DEBUG_TAP)
    if key in _CACHE:
        return _CACHE[key]
    cx = Ctx()
    cx.nc = bass.Bass("TRN2", target_bir_lowering=False)
    cx.prog = Prog()
    declare_dram(cx)
    with ExitStack() as stack:
        alloc_sbuf(cx, stack)
        emit_setup(cx)
        if "p0" in RUN_PHASES:
            cx.prog.fence()
            emit_p0(cx)
        if "rwkv" in RUN_PHASES:
            cx.prog.fence()
            emit_rwkv(cx)
        if "layers" in RUN_PHASES:
            for L in range(N_LAYERS):
                cx.prog.fence()
                emit_attn(cx, L)
                cx.prog.fence()
                emit_ffn(cx, L)
        cx.prog.fence()
        emit_combine(cx)
        emit_fence(cx)
        with cx.nc.Block() as block:
            @block.tensor
            def _(t):
                cx.prog.replay("pe", t)

            @block.scalar
            def _(s):
                cx.prog.replay("act", s)

            @block.vector
            def _(v):
                cx.prog.replay("dve", v)

            @block.gpsimd
            def _(g):
                cx.prog.replay("pool", g)

            @block.sync
            def _(sy):
                cx.prog.replay("sp", sy)
    _CACHE[key] = cx
    return cx


def kernel(**inputs):
    in_maps = host_prepare(inputs)
    cx = _build()
    res = run_bass_kernel_spmd(cx.nc, in_maps, core_ids=list(range(8)))
    return host_assemble(res.results)


# version 06 (final)
# speedup vs baseline: 1.1123x; 1.1123x over previous
import numpy as np
import ml_dtypes
import concourse.bass as bass
import concourse.mybir as mybir
from concourse.bass_utils import run_bass_kernel_spmd
from contextlib import ExitStack

F32 = mybir.dt.float32
BF16 = mybir.dt.bfloat16
AF = mybir.ActivationFunctionType
ALU = mybir.AluOpType
BF16NP = ml_dtypes.bfloat16

B, T, C = 4, 8192, 1024
NG = C // 128
DEPTH = 6
HEADS, DH = 16, 64
WIN = 256
DI = 2730
DI2 = 2 * DI
NPAIR = (DI + 127) // 128
LN_EPS = 1e-5
WKV_EPS = 1e-8
QK_SCALE = 8.0

FT = 4992
WARM = 1024
RT = FT + WARM
SPLIT = 4928
ODD_START = 3200

P0C = 512
RWC = 448
ATC = 896
FFC = 512
CMC = 512

NEG_MASK = -1.0e30

N_LAYERS = DEPTH
OUT_MODE = "full"
DEBUG_TAP = None
RUN_PHASES = ("p0", "rwkv", "layers", "combine")
SAME_ENGINE_SKIP = True


def _chunks(total, size):
    res, t = [], 0
    while t < total:
        res.append((t, min(size, total - t)))
        t += size
    return res


class Sem:
    __slots__ = ("name", "count", "h")

    def __init__(self, name):
        self.name, self.count, self.h = name, 0, None


class Res:
    __slots__ = ("name", "w", "readers")

    def __init__(self, name):
        self.name, self.w, self.readers = name, None, []


class OpRec:
    __slots__ = ("eng", "fn", "waits", "incs", "tag")

    def __init__(self, eng, fn, waits, incs, tag):
        self.eng, self.fn, self.waits, self.incs, self.tag = eng, fn, waits, incs, tag


ENGN = ("pe", "act", "dve", "pool", "sp")


class Prog:
    def __init__(self):
        self.ops = []
        self.esem = {e: Sem("e_" + e) for e in ENGN}
        self.RQ = 12
        self.qpool = {q: [Sem(f"q_{q}{i}") for i in range(self.RQ)]
                      for q in ("sp", "pool")}
        self.qn = {"sp": 0, "pool": 0}
        self.all_sems = list(self.esem.values()) + [s for v in self.qpool.values()
                                                    for s in v]
        self.fence_list = []
        self.fence_pending = set()
        self.deferred = {e: [] for e in ENGN}
        self.waited = {}

    def fence(self):
        for e in ENGN:
            if self.deferred[e]:
                self._force_inc(self.esem[e])
        self.fence_list = [[s, s.count] for s in self.all_sems if s.count > 0]
        self.fence_pending = set(ENGN) | {"q_sp", "q_pool"}

    def _force_inc(self, sem):
        raise RuntimeError(f"deferred mark on {sem.name} needed before any "
                           "covering inc: give that producer inc=True")

    def _mkwaits(self, eng, deps):
        best = {}
        own = self.esem.get(eng)
        for d in deps:
            if d is None:
                continue
            s, v = d[0], d[1]
            if s is own and (SAME_ENGINE_SKIP or eng == "pe"):
                continue
            if v is None:
                self._force_inc(s)
                v = d[1]
                assert v is not None
            if v <= 0:
                continue
            k = id(s)
            if best.get(k, (None, -1))[1] < v:
                best[k] = (s, v)
        out = []
        for s, v in best.values():
            assert v <= s.count, f"wait on future {s.name}:{v}>{s.count}"
            k = (eng, id(s))
            if eng != "#dma" and self.waited.get(k, -1) >= v:
                continue
            if eng != "#dma":
                self.waited[k] = v
            out.append((s, v))
        return out

    FORCE_INC_ALL = False

    def op(self, eng, fn, reads=(), writes=(), inc=True, extra=(), tag=""):
        if self.FORCE_INC_ALL:
            inc = True
        sem = self.esem[eng]
        deps = list(extra)
        if eng in self.fence_pending:
            self.fence_pending.discard(eng)
            deps.extend(self.fence_list)
        for r in reads:
            deps.append(r.w)
        for r in writes:
            deps.append(r.w)
            deps.extend(r.readers)
        waits = self._mkwaits(eng, deps)
        if inc:
            sem.count += 1
            incs = [(sem, 1)]
            mark = [sem, sem.count]
            for m in self.deferred[eng]:
                m[1] = sem.count
            self.deferred[eng] = []
        else:
            incs = []
            mark = [sem, None]
            self.deferred[eng].append(mark)
        self.ops.append(OpRec(eng, fn, waits, incs, tag))
        for r in reads:
            r.readers.append(mark)
        for r in writes:
            r.w = mark
            r.readers = []
        return mark

    def dma(self, q, out_ap, in_ap, reads=(), writes=(), extra=(), tag="dma",
            slow=False):
        slot = self.qn[q] % self.RQ
        self.qn[q] += 1
        qs = self.qpool[q][slot]
        deps = list(extra)
        fk = "q_" + q
        if fk in self.fence_pending:
            self.fence_pending.discard(fk)
            deps.extend(self.fence_list)
        if qs.count > 0:
            deps.append((qs, qs.count))
        for r in reads:
            deps.append(r.w)
        for r in writes:
            deps.append(r.w)
            deps.extend(r.readers)
        waits = self._mkwaits("q_" + q, deps)
        qs.count += 16
        self.ops.append(OpRec(
            q, lambda e, o=out_ap, i=in_ap, sl=slow: e.dma_start(
                out=o, in_=i, allow_slow_non_contiguous=sl),
            waits, [(qs, 16)], tag))
        mark = [qs, qs.count]
        for r in reads:
            r.readers.append(mark)
        for r in writes:
            r.w = mark
            r.readers = []
        return mark

    def raw_wait(self, eng, waits, tag=""):
        self.ops.append(OpRec(eng, None, list(waits), [], tag))

    def replay(self, eng_name, eng):
        for o in self.ops:
            if o.eng != eng_name:
                continue
            for s, v in o.waits:
                eng.wait_ge(s.h, v)
            instr = o.fn(eng) if o.fn is not None else None
            for s, n in o.incs:
                instr = instr.then_inc(s.h, n)


def _rotary_tables():
    n = T
    half = np.arange(0, DH, 2, dtype=np.float64)
    inv_freq = 1.0 / 10000.0 ** (half / DH)
    t = np.arange(n, dtype=np.float64)
    freqs = t[:, None] * inv_freq[None, :]
    base_scale = (half + 0.4 * DH) / (1.4 * DH)
    power = (t - n // 2) / (WIN // 2)
    scale = base_scale[None, :] ** power[:, None]
    cos, sin = np.cos(freqs), np.sin(freqs)

    def expand(a):
        return np.ascontiguousarray(np.concatenate([a, a], axis=1).T)

    def dup(a):
        return np.ascontiguousarray(np.concatenate([a, a], axis=0), dtype=np.float32)

    return (dup(expand(cos * scale)), dup(expand(sin * scale)),
            dup(expand(cos / scale)), dup(expand(sin / scale)))


def _mask_pattern(delta):
    r = np.arange(128)[:, None]
    c = np.arange(384)[None, :]
    d = delta + c - r
    return (((d < 0) | (d > WIN)).astype(np.float32) * NEG_MASK).astype(BF16NP)


def _perm_sigma():
    Pi = np.zeros((128, 128), dtype=np.float32)
    for blk in (0, 64):
        for m in range(0, 32):
            Pi[blk + m + 32, blk + m] = -1.0
        for m in range(32, 64):
            Pi[blk + m - 32, blk + m] = 1.0
    return Pi.astype(BF16NP)


def _blockdiag():
    p = np.zeros((128, 128), dtype=np.float32)
    p[0:64, 0:64] = 1.0
    p[64:128, 64:128] = 1.0
    return p.astype(BF16NP)


def host_prepare(inputs):
    x = np.asarray(inputs["L0_attn_x"])
    cs_q, ss_q, cs_k, ss_k = _rotary_tables()
    rs = lambda k: np.ascontiguousarray(np.asarray(inputs[k]).reshape(NG, 128))
    common = {
        "wk": np.ascontiguousarray(np.asarray(inputs["L0_attn_Wk"]).T),
        "wv": np.ascontiguousarray(np.asarray(inputs["L0_attn_Wv"]).T),
        "wr": np.ascontiguousarray(np.asarray(inputs["L0_attn_Wr"]).T),
        "wo": np.ascontiguousarray(np.asarray(inputs["L0_attn_Wo"]).T),
        "time_decay": rs("L0_attn_time_decay"),
        "tmk": rs("L0_attn_time_mix_k"),
        "tmv": rs("L0_attn_time_mix_v"),
        "tmr": rs("L0_attn_time_mix_r"),
        "ln_g": rs("L0_attn_ln_g"),
        "ln_b": rs("L0_attn_ln_b"),
        "f_wqkv": np.ascontiguousarray(inputs["L0_ffn_Wqkv"]),
        "f_wo": np.ascontiguousarray(inputs["L0_ffn_Wo"]),
        "f_w1": np.ascontiguousarray(inputs["L0_ffn_W1"]),
        "f_w2": np.ascontiguousarray(inputs["L0_ffn_W2"]),
        "f_qs": np.ascontiguousarray(inputs["L0_ffn_q_scale"]),
        "f_ks": np.ascontiguousarray(inputs["L0_ffn_k_scale"]),
        "f_lng": np.ascontiguousarray(np.asarray(inputs["L0_ffn_ln_g"]).reshape(DEPTH, NG, 128)),
        "f_lnb": np.ascontiguousarray(np.asarray(inputs["L0_ffn_ln_b"]).reshape(DEPTH, NG, 128)),
        "f_b1": np.ascontiguousarray(inputs["L0_ffn_b1"]),
        "f_b2": np.ascontiguousarray(np.asarray(inputs["L0_ffn_b2"]).reshape(DEPTH, NG, 128)),
        "pi_sigma": _perm_sigma(),
        "pat_blk": _blockdiag(),
        "pat_ones": np.full((128, 128), 1.0 / C, dtype=np.float32),
        "ident": np.eye(128, dtype=np.float32).astype(BF16NP),
        "mask0": _mask_pattern(0),
        "mask128": _mask_pattern(128),
        "mask256": _mask_pattern(256),
        "consts": np.tile(np.array([[LN_EPS, WKV_EPS, np.log(QK_SCALE), 1.0, 0.0]],
                                   dtype=np.float32), (128, 1)),
    }
    in_maps = []
    for core in range(8):
        b, h = core // 2, core % 2
        sf = 0 if h == 0 else ODD_START
        lo = sf - WARM
        xe = np.zeros((RT, C), dtype=np.float32)
        g0 = max(lo, 0)
        xe[g0 - lo:, :] = x[b, g0: sf + FT, :]
        mask = np.zeros((1, RT), dtype=np.float32)
        mask[0, g0 - lo:] = 1.0
        m = dict(common)
        m["x_ext"] = np.ascontiguousarray(xe.T)
        m["maskrow"] = mask
        m["cs_q"] = np.ascontiguousarray(cs_q[:, sf:sf + FT])
        m["ss_q"] = np.ascontiguousarray(ss_q[:, sf:sf + FT])
        m["cs_k"] = np.ascontiguousarray(cs_k[:, sf:sf + FT])
        m["ss_k"] = np.ascontiguousarray(ss_k[:, sf:sf + FT])
        in_maps.append(m)
    return in_maps


def host_assemble(results):
    out = np.empty((B, T, C), dtype=np.float32)
    for core in range(8):
        b, h = core // 2, core % 2
        o = np.asarray(results[core]["out"])
        if h == 0:
            out[b, 0:SPLIT] = o[:, 0:SPLIT].T
        else:
            out[b, SPLIT:T] = o[:, SPLIT - ODD_START:FT].T
    return out


class Ctx:
    pass


def cm_view(hbm_ap, t0, tlen):
    return hbm_ap[:, t0:t0 + tlen].rearrange("(g p) t -> p g t", p=128)


def sb3(view2d, tlen):
    return view2d.rearrange("p (g t) -> p g t", t=tlen)


A1_EL = 43680
A2_EL = 22528
A3_EL = 16896


class Arena:
    def __init__(self, nel, name):
        self.nel, self.name, self.cur = nel, name, 0

    def reset(self):
        self.cur = 0

    def take(self, n):
        o = self.cur
        self.cur += n
        assert self.cur <= self.nel, f"{self.name} overflow {self.cur}>{self.nel}"
        return o


def region(cx, key):
    r = cx.regions.get(key)
    if r is None:
        r = Res(str(key))
        cx.regions[key] = r
    return r


def a1v(cx, off, n):
    return cx.a1[:, off:off + n]


def a1v32(cx, off, n32):
    assert off % 2 == 0
    return cx.a1[:, off:off + 2 * n32].bitcast(F32)


def a2v(cx, off, n):
    return cx.a2[:, off:off + n]


def a2v32(cx, off, n32):
    assert off % 2 == 0
    return cx.a2[:, off:off + 2 * n32].bitcast(F32)


def a3v(cx, off, n):
    return cx.a3[:, off:off + n]


def a3vb(cx, off, n16):
    assert n16 % 2 == 0
    return cx.a3[:, off:off + n16 // 2].bitcast(BF16)


def declare_dram(cx):
    nc = cx.nc

    def din(name, shape, dt=F32):
        return nc.dram_tensor(name, list(shape), dt, kind="ExternalInput").ap()

    cx.x_ext = din("x_ext", (C, RT))
    cx.maskrow = din("maskrow", (1, RT))
    cx.wk = din("wk", (C, C)); cx.wv = din("wv", (C, C))
    cx.wr = din("wr", (C, C)); cx.wo = din("wo", (C, C))
    cx.time_decay = din("time_decay", (NG, 128))
    cx.tmk = din("tmk", (NG, 128)); cx.tmv = din("tmv", (NG, 128))
    cx.tmr = din("tmr", (NG, 128))
    cx.ln_g = din("ln_g", (NG, 128)); cx.ln_b = din("ln_b", (NG, 128))
    cx.f_wqkv = din("f_wqkv", (DEPTH, C, 3072))
    cx.f_wo = din("f_wo", (DEPTH, C, C))
    cx.f_w1 = din("f_w1", (DEPTH, C, DI2))
    cx.f_w2 = din("f_w2", (DEPTH, DI, C))
    cx.f_qs = din("f_qs", (DEPTH, DH)); cx.f_ks = din("f_ks", (DEPTH, DH))
    cx.f_lng = din("f_lng", (DEPTH, NG, 128))
    cx.f_lnb = din("f_lnb", (DEPTH, NG, 128))
    cx.f_b1 = din("f_b1", (DEPTH, DI2))
    cx.f_b2 = din("f_b2", (DEPTH, NG, 128))
    cx.pi_sigma = din("pi_sigma", (128, 128), BF16)
    cx.pat_blk = din("pat_blk", (128, 128), BF16)
    cx.pat_ones = din("pat_ones", (128, 128))
    cx.ident = din("ident", (128, 128), BF16)
    cx.masks_in = {0: din("mask0", (128, 384), BF16),
                   128: din("mask128", (128, 384), BF16),
                   256: din("mask256", (128, 384), BF16)}
    cx.cs_q = din("cs_q", (128, FT)); cx.ss_q = din("ss_q", (128, FT))
    cx.cs_k = din("cs_k", (128, FT)); cx.ss_k = din("ss_k", (128, FT))
    cx.consts = din("consts", (128, 5))

    cx.out_hbm = nc.dram_tensor("out", [C, FT], F32, kind="ExternalOutput").ap()
    cx.xn_hbm = nc.dram_tensor("xn_scr", [C, RT + 1], F32, kind="Internal").ap()
    cx.s_hbm = nc.dram_tensor("s_scr", [C, FT], F32, kind="Internal").ap()
    cx.sa_hbm = nc.dram_tensor("sa_scr", [C, FT + 1], F32, kind="Internal").ap()
    cx.sb_hbm = nc.dram_tensor("sb_scr", [C, FT + 1], F32, kind="Internal").ap()


def alloc_sbuf(cx, stack):
    nc = cx.nc
    e = stack.enter_context
    cx.a1 = e(nc.sbuf_tensor("a1", [128, A1_EL], BF16))
    cx.a2 = e(nc.sbuf_tensor("a2", [128, A2_EL], BF16))
    cx.a3 = e(nc.sbuf_tensor("a3", [128, A3_EL], F32))
    cx.pi_sb = e(nc.sbuf_tensor("pi_sb", [128, 128], BF16))
    cx.patb_sb = e(nc.sbuf_tensor("patb_sb", [128, 128], BF16))
    cx.pato_sb = e(nc.sbuf_tensor("pato_sb", [128, 128], F32))
    cx.ident_sb = e(nc.sbuf_tensor("ident_sb", [128, 128], BF16))
    cx.ones_sb = e(nc.sbuf_tensor("ones_sb", [128, 128], BF16))
    cx.zrow_sb = e(nc.sbuf_tensor("zrow_sb", [1, 512], BF16))
    cx.onesf_sb = e(nc.sbuf_tensor("onesf_sb", [1, 128], F32))
    cx.mask_sb = {d: e(nc.sbuf_tensor(f"mask{d}_sb", [128, 384], BF16))
                  for d in (0, 128, 256)}
    cx.consts_sb = e(nc.sbuf_tensor("consts_sb", [128, 5], F32))
    cx.cols = e(nc.sbuf_tensor("cols", [128, 160], F32))
    cx.psum = [e(nc.psum_tensor(f"ps{i}", [128, 512], F32)) for i in range(8)]
    for s in cx.prog.all_sems:
        s.h = e(nc.semaphore(s.name))
    cx.bank = [Res(f"bank{i}") for i in range(8)]
    cx.regions = {}
    cx.r_hbm = {n: Res("hbm_" + n) for n in ("xn", "s", "sa", "sb", "out")}
    cx.rc = Res("cols")
    cx.gctr = 0
    cx.A1 = Arena(A1_EL, "a1")
    cx.A2 = Arena(A2_EL, "a2")
    cx.A3 = Arena(A3_EL, "a3")


def emit_setup(cx):
    p = cx.prog
    for ap_sb, src in ((cx.pi_sb, cx.pi_sigma), (cx.patb_sb, cx.pat_blk),
                       (cx.pato_sb, cx.pat_ones), (cx.ident_sb, cx.ident),
                       (cx.consts_sb, cx.consts)):
        p.dma("sp", ap_sb[:, :], src, writes=[region(cx, ("fix", ap_sb.name))],
              tag="ld_fix")
    for d in (0, 128, 256):
        p.dma("sp", cx.mask_sb[d][:, :], cx.masks_in[d],
              writes=[region(cx, ("fix", f"mask{d}"))], tag="ld_mask")
    p.op("dve", lambda e: e.memset(cx.ones_sb[:, :], 1.0),
         writes=[region(cx, ("fix", "ones"))], tag="ones")
    p.op("dve", lambda e: e.memset(cx.zrow_sb[:, :], 0.0),
         writes=[region(cx, ("fix", "zrow"))], tag="zrow")
    p.op("dve", lambda e: e.memset(cx.onesf_sb[:, :], 1.0),
         writes=[region(cx, ("fix", "onesf"))], tag="onesf")
    for j in range(NG):
        for base, src in ((0, cx.ln_g), (8, cx.ln_b), (16, cx.tmk),
                          (24, cx.tmv), (32, cx.tmr), (40, cx.time_decay)):
            p.dma("sp", cx.cols[:, base + j:base + j + 1],
                  src[j:j + 1, :].rearrange("a p -> p a"),
                  writes=[cx.rc], tag="ld_col")
    p.op("act", lambda e: e.activation(cx.cols[:, 40:48], cx.cols[:, 40:48], AF.Exp),
         reads=[cx.rc], writes=[cx.rc], tag="td_exp")
    zr = region(cx, ("a3", "zrow"))
    p.op("dve", lambda e: e.memset(a3v(cx, 0, NG), 0.0), writes=[zr], tag="zc")
    for name, hbm in (("xn", cx.xn_hbm), ("sa", cx.sa_hbm), ("sb", cx.sb_hbm)):
        for g in range(NG):
            p.dma("sp", hbm[128 * g:128 * (g + 1), 0:1], cx.a3[:, g:g + 1],
                  reads=[zr], writes=[cx.r_hbm[name]], tag="zg_" + name,
                  slow=True)


def emit_ln(cx, tag, L, r_src, src_off, writer, bm_i, bv_i, scr_off):
    p = cx.prog
    bm, bv = cx.psum[bm_i], cx.psum[bv_i]
    rbm, rbv = cx.bank[bm_i], cx.bank[bv_i]
    m2 = a3v(cx, scr_off, L)
    varb = a3v(cx, scr_off + L, L)
    rstdb = a3v(cx, scr_off + 2 * L, L)
    sq = a3v(cx, scr_off + 3 * L, L)
    t2 = a3v(cx, scr_off + 4 * L, L)
    r_m2 = region(cx, ("a3", scr_off))
    r_varb = region(cx, ("a3", scr_off + L))
    r_rstd = region(cx, ("a3", scr_off + 2 * L))
    r_sq = region(cx, ("a3", scr_off + 3 * L))
    r_t2 = region(cx, ("a3", scr_off + 4 * L))
    eps = cx.consts_sb[:, 0:1]
    for g in range(NG):
        xg = a3v(cx, src_off + g * L, L)
        p.op("act", lambda e, o=sq, i=xg: e.activation(o, i, AF.Square),
             reads=[r_src], writes=[r_sq], tag=tag + "_sq")
        for c0 in range(0, L, 512):
            w = min(512, L - c0)
            st, sp_ = (g == 0), (g == NG - 1)
            p.op("pe", lambda e, o=bm[:, c0:c0 + w], r=xg[:, c0:c0 + w],
                 st=st, sp=sp_: e.matmul(o, cx.pato_sb[:, :], r, start=st, stop=sp),
                 reads=[r_src], writes=[rbm], inc=True, tag=tag + "_mm")
            p.op("pe", lambda e, o=bv[:, c0:c0 + w], r=sq[:, c0:c0 + w],
                 st=st, sp=sp_: e.matmul(o, cx.pato_sb[:, :], r, start=st, stop=sp),
                 reads=[r_sq], writes=[rbv], inc=True, tag=tag + "_mv")
    p.op("act", lambda e: e.activation(m2, bm[:, 0:L], AF.Square),
         reads=[rbm], writes=[r_m2], tag=tag + "_m2")
    p.op("dve", lambda e: e.tensor_tensor(varb, bv[:, 0:L], m2, ALU.subtract),
         reads=[rbv, r_m2], writes=[r_varb], tag=tag + "_var")
    p.op("act", lambda e: e.activation(rstdb, varb, AF.Ln, bias=eps),
         reads=[r_varb], writes=[r_rstd], tag=tag + "_lnv")
    p.op("act", lambda e: e.activation(rstdb, rstdb, AF.Exp, scale=-0.5),
         reads=[r_rstd], writes=[r_rstd], tag=tag + "_expv")
    for g in range(NG):
        xg = a3v(cx, src_off + g * L, L)
        p.op("dve", lambda e, i=xg: e.tensor_tensor(sq, i, bm[:, 0:L], ALU.subtract),
             reads=[r_src, rbm], writes=[r_sq], tag=tag + "_sub")
        p.op("dve", lambda e: e.tensor_tensor(t2, sq, rstdb, ALU.mult),
             reads=[r_sq, r_rstd], writes=[r_t2], tag=tag + "_mul")
        writer(g, t2, r_t2)


def emit_p0(cx):
    p = cx.prog
    A = cx.A3
    A.reset()
    XA = A.take(NG * P0C)
    OUT = A.take(NG * P0C)
    SCR = A.take(5 * P0C)
    r_xa = region(cx, ("a3", "p0xa"))
    r_out = region(cx, ("a3", "p0out"))
    for (t0, L) in _chunks(RT, P0C):
        p.dma("sp", sb3(a3v(cx, XA, NG * L), L), cm_view(cx.x_ext, t0, L),
              writes=[r_xa], tag=f"p0_ld@{t0}")

        def writer(g, t2, r_t2, L=L):
            p.op("dve", lambda e, g=g, t2=t2: e.tensor_copy(a3v(cx, OUT + g * L, L), t2),
                 reads=[r_t2], writes=[r_out], tag="p0_o")

        emit_ln(cx, f"p0@{t0}", L, r_xa, XA, writer, 6, 7, SCR)
        p.dma("sp", cm_view(cx.xn_hbm, 1 + t0, L), sb3(a3v(cx, OUT, NG * L), L),
              reads=[r_out], writes=[cx.r_hbm["xn"]], tag=f"p0_st@{t0}")


def emit_rwkv(cx):
    p = cx.prog
    A1, A2, A3 = cx.A1, cx.A2, cx.A3
    A1.reset(); A2.reset(); A3.reset()

    woff = {}
    wres = {}
    for name, hbm in (("wk", cx.wk), ("wv", cx.wv), ("wr", cx.wr), ("wo", cx.wo)):
        off = A1.take(NG * C)
        woff[name] = off
        rw = region(cx, ("a1", off))
        wres[name] = rw
        for g in range(NG):
            p.dma("pool", a1v(cx, off + g * C, C), hbm[128 * g:128 * (g + 1), :],
                  writes=[rw], tag=f"rw_ldw_{name}")
    SS0 = A1.take(2 * RWC)
    SS1 = A1.take(2 * RWC)
    EKV = A1.take(2 * NG * RWC)
    r_ss = [region(cx, ("a1", SS0)), region(cx, ("a1", SS1))]
    r_ekv = region(cx, ("a1", EKV))

    VO = A2.take(2 * NG * RWC)
    SRO = A2.take(2 * NG * RWC)
    DO = A2.take(2 * NG * RWC)
    r_v = region(cx, ("a2", VO))
    r_sr = region(cx, ("a2", SRO))
    r_den = region(cx, ("a2", DO))
    vv = a2v32(cx, VO, NG * RWC)
    srv = a2v32(cx, SRO, NG * RWC)
    denv = a2v32(cx, DO, NG * RWC)
    ekvv = a1v32(cx, EKV, NG * RWC)

    XN = A3.take(NG * (RWC + 1))
    XN2 = A3.take(NG * (RWC + 1))
    EK = A3.take(NG * RWC)
    SCR = A3.take(5 * (RWC + 1))
    XMB = A3.take((NG * RWC + 1) // 2)
    MRB = A3.take(RWC + 1)
    r_xn = region(cx, ("a3", XN))
    r_xn2 = region(cx, ("a3", XN2))
    r_ek = region(cx, ("a3", EK))
    r_xm = region(cx, ("a3", XMB))
    r_mb = region(cx, ("a3", MRB))

    cden = cx.cols[:, 76:84]
    cnum = cx.cols[:, 84:92]
    r_carry = region(cx, ("cols", 76))
    p.op("dve", lambda e: e.memset(cx.cols[:, 76:92], 0.0), writes=[r_carry],
         tag="rw_c0")

    col = lambda j: cx.cols[:, j:j + 1]

    for (u0, L) in _chunks(RT, RWC):
        p.dma("sp", sb3(a3v(cx, XN, NG * (L + 1)), L + 1),
              cm_view(cx.xn_hbm, u0, L + 1),
              writes=[r_xn], tag=f"rw_ldxn@{u0}")

        L1 = L + 1

        def wr_xn2(g, t2, r_t2, L1=L1):
            p.op("dve", lambda e, g=g, t2=t2: e.tensor_scalar(
                a3v(cx, XN2 + g * L1, L1), t2, col(0 + g), col(8 + g),
                ALU.mult, ALU.add),
                reads=[r_t2, cx.rc], writes=[r_xn2], tag="rw_xn2")

        emit_ln(cx, f"rwln@{u0}", L1, r_xn, XN, wr_xn2, 6, 7, SCR)

        need_mask = u0 < WARM
        if need_mask:
            bmk, rbmk = cx.psum[5], cx.bank[5]
            mrow = a3vb(cx, SCR, 512)
            r_mrow = region(cx, ("a3", SCR))
            m0 = max(u0 - 1, 0)
            mlen = u0 + L - m0
            moff = 1 if u0 == 0 else 0
            p.dma("pool", mrow[0:1, 0:mlen], cx.maskrow[:, m0:m0 + mlen],
                  writes=[r_mrow], tag="rw_ldmrow")
            for c0 in range(0, mlen, 512):
                w = min(512, mlen - c0)
                p.op("pe", lambda e, o=bmk[:, c0:c0 + w],
                     r=mrow[0:1, c0:c0 + w]:
                     e.matmul(o, cx.ones_sb[0:1, :], r, start=True, stop=True),
                     reads=[r_mrow], writes=[rbmk],
                     inc=(c0 + 512 >= mlen), tag="rw_mb_mm")
            p.op("dve", lambda e, o=a3v(cx, MRB + moff, mlen), i=bmk[:, 0:mlen]:
                 e.tensor_copy(o, i),
                 reads=[rbmk], writes=[r_mb], tag="rw_mb")
            if u0 == 0:
                p.op("dve", lambda e: e.memset(a3v(cx, MRB, 1), 0.0),
                     writes=[r_mb], tag="rw_mb0")
            for g in range(NG):
                p.op("dve", lambda e, a=a3v(cx, XN2 + g * L1, L1),
                     m=a3v(cx, MRB, L1): e.tensor_tensor(a, a, m, ALU.mult),
                     reads=[r_xn2, r_mb], writes=[r_xn2], tag="rw_mxn2")

        xm = a3vb(cx, XMB, NG * L if (NG * L) % 2 == 0 else NG * L + 1)

        def mix(tm_base):
            for g in range(NG):
                cur = a3v(cx, XN2 + g * L1 + 1, L)
                prev = a3v(cx, XN2 + g * L1, L)
                dscr = a3v(cx, SCR + 4 * L1, L)
                r_d = region(cx, ("a3", SCR + 4 * L1))
                p.op("dve", lambda e, a=cur, b=prev, o=dscr:
                     e.tensor_tensor(o, a, b, ALU.subtract),
                     reads=[r_xn2], writes=[r_d], tag="rw_dm")
                p.op("dve", lambda e, o=xm[:, g * L:(g + 1) * L], i=dscr,
                     s=col(tm_base + g), b=prev:
                     e.scalar_tensor_tensor(o, i, s, b, ALU.mult, ALU.add),
                     reads=[r_d, r_xn2, cx.rc], writes=[r_xm], tag="rw_mix")

        def matmul_evac(wname, evac):
            off = woff[wname]
            rw = wres[wname]
            for og in range(NG):
                bk, rbk = cx.psum[og % 2], cx.bank[og % 2]
                for c0 in range(0, L, 512):
                    w = min(512, L - c0)
                    for ig in range(NG):
                        p.op("pe", lambda e, o=bk[:, c0:c0 + w],
                             l=a1v(cx, off + ig * C + og * 128, 128),
                             r=xm[:, ig * L + c0:ig * L + c0 + w],
                             st=(ig == 0), sp=(ig == NG - 1):
                             e.matmul(o, l, r, start=st, stop=sp),
                             reads=[r_xm, rw], writes=[rbk], inc=(ig == NG - 1),
                             tag=f"rw_mm_{wname}")
                    evac(og, c0, w, bk, rbk)

        def evac_ek(og, c0, w, bk, rbk):
            p.op("act", lambda e, o=a3v(cx, EK + og * L + c0, w), i=bk[:, c0:c0 + w]:
                 e.activation(o, i, AF.Exp), reads=[rbk], writes=[r_ek], tag="rw_ek")

        def evac_v(og, c0, w, bk, rbk):
            p.op("act", lambda e, o=vv[:, og * L + c0:og * L + c0 + w],
                 i=bk[:, c0:c0 + w]: e.activation(o, i, AF.Copy),
                 reads=[rbk], writes=[r_v], tag="rw_v")

        def evac_sr(og, c0, w, bk, rbk):
            p.op("act", lambda e, o=srv[:, og * L + c0:og * L + c0 + w],
                 i=bk[:, c0:c0 + w]: e.activation(o, i, AF.Sigmoid),
                 reads=[rbk], writes=[r_sr], tag="rw_sr")

        mix(16); matmul_evac("wk", evac_ek)
        mix(24); matmul_evac("wv", evac_v)
        mix(32); matmul_evac("wr", evac_sr)

        if need_mask:
            for g in range(NG):
                p.op("dve", lambda e, a=a3v(cx, EK + g * L, L),
                     m=a3v(cx, MRB + 1, L): e.tensor_tensor(a, a, m, ALU.mult),
                     reads=[r_ek, r_mb], writes=[r_ek], tag="rw_mek")
                p.op("dve", lambda e, a=vv[:, g * L:(g + 1) * L],
                     m=a3v(cx, MRB + 1, L): e.tensor_tensor(a, a, m, ALU.mult),
                     reads=[r_v, r_mb], writes=[r_v], tag="rw_mv")

        for g in range(NG):
            p.op("dve", lambda e, o=ekvv[:, g * L:(g + 1) * L],
                 a=a3v(cx, EK + g * L, L), b=vv[:, g * L:(g + 1) * L]:
                 e.tensor_tensor(o, a, b, ALU.mult),
                 reads=[r_ek, r_v], writes=[r_ekv], tag="rw_ekv")

        for g in range(NG):
            p.op("dve", lambda e, o=denv[:, g * L:(g + 1) * L],
                 tdb=col(40 + g).broadcast_to([128, L]),
                 d1=a3v(cx, EK + g * L, L), c0_=cden[:, g:g + 1]:
                 e.tensor_tensor_scan(o, tdb, d1, c0_, ALU.mult, ALU.add),
                 reads=[r_ek, r_carry, cx.rc], writes=[r_den], tag="rw_sd")
        for g in range(NG):
            p.op("dve", lambda e, o=a3v(cx, EK + g * L, L),
                 tdb=col(40 + g).broadcast_to([128, L]),
                 d1=ekvv[:, g * L:(g + 1) * L], c0_=cnum[:, g:g + 1]:
                 e.tensor_tensor_scan(o, tdb, d1, c0_, ALU.mult, ALU.add),
                 reads=[r_ekv, r_carry], writes=[r_ek], tag="rw_sn")
        for g in range(NG):
            p.op("dve", lambda e, o=cden[:, g:g + 1],
                 i=denv[:, g * L + L - 1:g * L + L]: e.tensor_copy(o, i),
                 reads=[r_den], writes=[r_carry], tag="rw_cd")
            p.op("dve", lambda e, o=cnum[:, g:g + 1],
                 i=a3v(cx, EK + g * L + L - 1, 1): e.tensor_copy(o, i),
                 reads=[r_ek], writes=[r_carry], tag="rw_cn")

        p.op("act", lambda e, n=NG * L: e.activation(denv[:, 0:n], denv[:, 0:n],
                                                     AF.Ln, bias=cx.consts_sb[:, 1:2]),
             reads=[r_den], writes=[r_den], tag="rw_lnd")
        p.op("act", lambda e, n=NG * L: e.activation(denv[:, 0:n], denv[:, 0:n],
                                                     AF.Exp, scale=-1.0),
             reads=[r_den], writes=[r_den], tag="rw_exd")
        sgt = xm
        for g in range(NG):
            p.op("dve", lambda e, o=vv[:, g * L:(g + 1) * L],
                 a=a3v(cx, EK + g * L, L), b=denv[:, g * L:(g + 1) * L]:
                 e.tensor_tensor(o, a, b, ALU.mult),
                 reads=[r_ek, r_den], writes=[r_v], tag="rw_wkv")
            p.op("dve", lambda e, o=sgt[:, g * L:(g + 1) * L],
                 a=vv[:, g * L:(g + 1) * L], b=srv[:, g * L:(g + 1) * L]:
                 e.tensor_tensor(o, a, b, ALU.mult),
                 reads=[r_v, r_sr], writes=[r_xm], tag="rw_sgt")

        if u0 + L > WARM:
            off = woff["wo"]
            rw = wres["wo"]
            lo = max(WARM, u0)
            o0 = lo - u0
            for og in range(NG):
                bk, rbk = cx.psum[2 + og % 2], cx.bank[2 + og % 2]
                for c0 in range(0, L, 512):
                    w = min(512, L - c0)
                    for ig in range(NG):
                        p.op("pe", lambda e, o=bk[:, c0:c0 + w],
                             l=a1v(cx, off + ig * C + og * 128, 128),
                             r=sgt[:, ig * L + c0:ig * L + c0 + w],
                             st=(ig == 0), sp=(ig == NG - 1):
                             e.matmul(o, l, r, start=st, stop=sp),
                             reads=[r_xm, rw], writes=[rbk], inc=(ig == NG - 1),
                             tag="rw_mm_s")
                sl = og % 2
                ssv = a1v32(cx, SS0 if sl == 0 else SS1, RWC)
                p.op("dve", lambda e, o=ssv[:, 0:L], i=bk[:, 0:L]: e.tensor_copy(o, i),
                     reads=[rbk], writes=[r_ss[sl]], tag="rw_sev")
                p.dma("sp",
                      cx.s_hbm[128 * og:128 * (og + 1),
                               u0 - WARM + o0:u0 - WARM + L],
                      ssv[:, o0:L], reads=[r_ss[sl]], writes=[cx.r_hbm["s"]],
                      tag="rw_sst")


def stream_pair(cx, layer):
    if layer == 0:
        return cx.xn_hbm, 1 + WARM, cx.sa_hbm
    return cx.sb_hbm, 1, cx.sa_hbm


def hbm_res_of(cx, hbm):
    if hbm is cx.xn_hbm:
        return cx.r_hbm["xn"]
    if hbm is cx.sa_hbm:
        return cx.r_hbm["sa"]
    if hbm is cx.sb_hbm:
        return cx.r_hbm["sb"]
    if hbm is cx.s_hbm:
        return cx.r_hbm["s"]
    return cx.r_hbm["out"]


def emit_attn(cx, layer):
    p = cx.prog
    src, sbase, dst = stream_pair(cx, layer)
    r_srcalifc = hbm_res_of(cx, src)
    r_dst = hbm_res_of(cx, dst)
    A1, A2, A3 = cx.A1, cx.A2, cx.A3
    A1.reset(); A2.reset(); A3.reset()

    XSB = A1.take(NG * ATC)
    QQ = A1.take(NG * ATC)
    KK = A1.take(NG * (256 + ATC))
    VT = A1.take((256 + ATC) // 128 * 1024)
    AO = A1.take(NG * ATC)
    r_xsb = region(cx, ("a1", XSB))
    r_qq = region(cx, ("a1", QQ))
    r_kk = region(cx, ("a1", KK))
    r_vt = region(cx, ("a1", VT))
    r_ao = region(cx, ("a1", AO))

    WOSB = A2.take(NG * C)
    WQ0 = A2.take(NG * 512)
    WQ1 = A2.take(NG * 512)
    EPT0 = A2.take(384)
    EPT1 = A2.take(384)
    r_wo = region(cx, ("a2", WOSB))
    r_wq = [region(cx, ("a2", WQ0)), region(cx, ("a2", WQ1))]
    r_ept = [region(cx, ("a2", EPT0)), region(cx, ("a2", EPT1))]

    TCQ = A3.take(2 * ATC); TSQ = A3.take(2 * ATC)
    TCK = A3.take(2 * ATC); TSK = A3.take(2 * ATC)
    QE0 = A3.take(256); QE1 = A3.take(256)
    SQ0 = A3.take(256); SQ1 = A3.take(256)
    T0B0 = A3.take(256); T0B1 = A3.take(256)
    T1_ = A3.take(512); T2_ = A3.take(512)
    T1b_ = A3.take(512); T2b_ = A3.take(512)
    RQB0 = A3.take(512); RQB1 = A3.take(512)
    XRES = A3.take(NG * 512)
    RECIP = A3.take(512)
    r_tab2 = [region(cx, ("a3", "attab0")), region(cx, ("a3", "attab1"))]
    r_xres = region(cx, ("a3", XRES))
    r_recip = region(cx, ("a3", RECIP))

    r_qsrow = region(cx, ("a3", "qsrow"))
    qsrow = a3v(cx, T1_, DH)
    ksrow = a3v(cx, T1_ + DH, DH)
    mprod = a3v(cx, T1_ + 2 * DH, 1)
    for half in (0, 1):
        p.dma("sp", cx.cols[64 * half:64 * (half + 1), 72:73],
              cx.f_qs[layer:layer + 1, :].rearrange("a p -> p a"),
              writes=[cx.rc], tag="at_qscol")
        p.dma("sp", cx.cols[64 * half:64 * (half + 1), 73:74],
              cx.f_ks[layer:layer + 1, :].rearrange("a p -> p a"),
              writes=[cx.rc], tag="at_kscol")
    p.dma("sp", qsrow[0:1, :], cx.f_qs[layer:layer + 1, :], writes=[r_qsrow],
          tag="at_qsrow")
    p.dma("sp", ksrow[0:1, :], cx.f_ks[layer:layer + 1, :], writes=[r_qsrow],
          tag="at_ksrow")
    p.op("dve", lambda e: e.tensor_reduce(qsrow[0:1, 0:1], qsrow[0:1, :],
                                          mybir.AxisListType.X, ALU.max,
                                          apply_absolute_value=True),
         reads=[r_qsrow], writes=[r_qsrow], tag="at_qsmax")
    p.op("dve", lambda e: e.tensor_reduce(ksrow[0:1, 0:1], ksrow[0:1, :],
                                          mybir.AxisListType.X, ALU.max,
                                          apply_absolute_value=True),
         reads=[r_qsrow], writes=[r_qsrow], tag="at_ksmax")
    p.op("dve", lambda e: e.tensor_tensor(mprod[0:1, :], qsrow[0:1, 0:1],
                                          ksrow[0:1, 0:1], ALU.mult),
         reads=[r_qsrow], writes=[r_qsrow], tag="at_mprod")
    bn6 = cx.psum[6]
    p.op("pe", lambda e: e.matmul(bn6[:, 0:1], cx.onesf_sb[:, :], mprod[0:1, :],
                                  start=True, stop=True),
         reads=[r_qsrow, region(cx, ("fix", "onesf"))], writes=[cx.bank[6]],
         tag="at_negb_mm")
    p.op("act", lambda e: e.activation(cx.cols[:, 74:75], bn6[:, 0:1], AF.Copy,
                                       scale=-QK_SCALE),
         reads=[cx.bank[6]], writes=[cx.rc], tag="at_negb")

    for g in range(NG):
        p.dma("pool", a2v(cx, WOSB + g * C, C),
              cx.f_wo[layer, 128 * g:128 * (g + 1), :], writes=[r_wo],
              tag="at_ldwo")

    n_sl = lambda L: [(c0, min(512, L - c0)) for c0 in range(0, L, 512)]

    for ci, (t0, L) in enumerate(_chunks(FT, ATC)):
        tpar = ci % 2
        r_tab = r_tab2[tpar]
        toff = tpar * ATC
        for dst_off, srcap in ((TCQ, cx.cs_q), (TSQ, cx.ss_q),
                               (TCK, cx.cs_k), (TSK, cx.ss_k)):
            p.dma("sp", a3v(cx, dst_off + toff, L), srcap[:, t0:t0 + L],
                  writes=[r_tab], tag="at_ldtab")
        p.dma("pool", sb3(a1v(cx, XSB, 4 * L), L),
              src[0:512, sbase + t0:sbase + t0 + L].rearrange(
                  "(g p) t -> p g t", p=128),
              writes=[r_xsb], tag="at_ldxs_a")
        p.dma("pool", sb3(a1v(cx, XSB + 4 * L, 4 * L), L),
              src[512:1024, sbase + t0 - 1:sbase + t0 - 1 + L].rearrange(
                  "(g p) t -> p g t", p=128),
              writes=[r_xsb], tag="at_ldxs_s")

        for ot in range(16):
            is_q = ot < 8
            g = ot % 8
            wq_off = (WQ0, WQ1)[ot % 2]
            rwq = r_wq[ot % 2]
            p.dma("pool", sb3(a2v(cx, wq_off, NG * 128), 128),
                  cx.f_wqkv[layer, :, 128 * ot:128 * (ot + 1)].rearrange(
                      "(g p) oc -> p g oc", p=128),
                  writes=[rwq], tag="at_ldwq")
            for c0, w in n_sl(L):
                gpar = cx.gctr % 2
                cx.gctr += 1
                bk_i = gpar
                bk, rbk = cx.psum[bk_i], cx.bank[bk_i]
                for ig in range(NG):
                    p.op("pe", lambda e, o=bk[:, 0:w],
                         l=a2v(cx, wq_off + ig * 128, 128),
                         r=a1v(cx, XSB + ig * L + c0, w),
                         st=(ig == 0), sp=(ig == NG - 1):
                         e.matmul(o, l, r, start=st, stop=sp),
                         reads=[r_xsb, rwq], writes=[rbk], inc=(ig == NG - 1),
                         tag="at_mm_qk")
                par = gpar
                qe = a3vb(cx, (QE0, QE1)[par], 512)[:, 0:w]
                sq = a3vb(cx, (SQ0, SQ1)[par], 512)[:, 0:w]
                t0b = a3vb(cx, (T0B0, T0B1)[par], 512)[:, 0:w]
                rqb = a3v(cx, (RQB0, RQB1)[par], w)
                t1 = a3v(cx, (T1_, T1b_)[par], w)
                t2 = a3v(cx, (T2_, T2b_)[par], w)
                r_qe = region(cx, ("a3", (QE0, QE1)[par]))
                r_sq = region(cx, ("a3", (SQ0, SQ1)[par]))
                r_t0b = region(cx, ("a3", (T0B0, T0B1)[par]))
                r_rqb = region(cx, ("a3", (RQB0, RQB1)[par]))
                r_t1 = region(cx, ("a3", (T1_, T1b_)[par]))
                r_t2 = region(cx, ("a3", (T2_, T2b_)[par]))
                p.op("act", lambda e, o=qe, i=bk[:, 0:w]: e.activation(o, i, AF.Copy),
                     reads=[rbk], writes=[r_qe], tag="at_evq")
                p.op("dve", lambda e, o=sq, i=qe: e.tensor_tensor(o, i, i, ALU.mult),
                     reads=[r_qe], writes=[r_sq], tag="at_sq")
                bs, rbs = cx.psum[4 + par], cx.bank[4 + par]
                p.op("pe", lambda e, o=bs[:, 0:w], r=sq:
                     e.matmul(o, cx.patb_sb[:, :], r, start=True, stop=True),
                     reads=[r_sq, region(cx, ("fix", "patb_sb"))], writes=[rbs],
                     tag="at_ssb")
                if is_q:
                    p.op("act", lambda e, o=rqb, i=bs[:, 0:w]:
                         e.activation(o, i, AF.Ln), reads=[rbs], writes=[r_rqb],
                         tag="at_lnq")
                    p.op("act", lambda e, o=rqb, i=rqb:
                         e.activation(o, i, AF.Exp, scale=-0.5,
                                      bias=cx.consts_sb[:, 2:3]),
                         reads=[r_rqb], writes=[r_rqb], tag="at_exq")
                else:
                    p.op("act", lambda e, o=rqb, i=bs[:, 0:w]:
                         e.activation(o, i, AF.Ln), reads=[rbs], writes=[r_rqb],
                         tag="at_lnk")
                    p.op("act", lambda e, o=rqb, i=rqb:
                         e.activation(o, i, AF.Exp, scale=-0.5),
                         reads=[r_rqb], writes=[r_rqb], tag="at_exk")
                scol = cx.cols[:, 72:73] if is_q else cx.cols[:, 73:74]
                p.op("dve", lambda e, o=t0b, i=qe, s=scol, b=rqb:
                     e.scalar_tensor_tensor(o, i, s, b, ALU.mult, ALU.mult),
                     reads=[r_qe, r_rqb, cx.rc], writes=[r_t0b], tag="at_t0")
                bp, rbp = cx.psum[6 + par], cx.bank[6 + par]
                p.op("pe", lambda e, o=bp[:, 0:w], r=t0b:
                     e.matmul(o, cx.pi_sb[:, :], r, start=True, stop=True),
                     reads=[r_t0b, region(cx, ("fix", "pi_sb"))], writes=[rbp],
                     tag="at_pi")
                cst = a3v(cx, (TCQ if is_q else TCK) + toff + c0, w)
                sst = a3v(cx, (TSQ if is_q else TSK) + toff + c0, w)
                p.op("dve", lambda e, o=t1, a=t0b, b=cst:
                     e.tensor_tensor(o, a, b, ALU.mult),
                     reads=[r_t0b, r_tab], writes=[r_t1], tag="at_t1")
                p.op("dve", lambda e, o=t2, a=bp[:, 0:w], b=sst:
                     e.tensor_tensor(o, a, b, ALU.mult),
                     reads=[rbp, r_tab], writes=[r_t2], tag="at_t2")
                dst_off = (QQ + g * L + c0) if is_q else (KK + g * (256 + ATC) + 256 + c0)
                dst_res = r_qq if is_q else r_kk
                p.op("dve", lambda e, o=a1v(cx, dst_off, w), a=t1, b=t2:
                     e.tensor_tensor(o, a, b, ALU.add),
                     reads=[r_t1, r_t2], writes=[dst_res], tag="at_qk")

        ntt = L // 128
        for h2 in range(2):
            wq_off = (WQ0, WQ1)[h2]
            rwq = r_wq[h2]
            p.dma("pool", sb3(a2v(cx, wq_off, NG * 512), 512),
                  cx.f_wqkv[layer, :, 2048 + 512 * h2:2048 + 512 * (h2 + 1)].rearrange(
                      "(g p) oc -> p g oc", p=128),
                  writes=[rwq], tag="at_ldwv")
            for tt in range(ntt):
                bk_i = 2 + tt % 2
                bk, rbk = cx.psum[bk_i], cx.bank[bk_i]
                for ig in range(NG):
                    p.op("pe", lambda e, o=bk[:, 0:512],
                         l=a1v(cx, XSB + ig * L + tt * 128, 128),
                         r=a2v(cx, wq_off + ig * 512, 512),
                         st=(ig == 0), sp=(ig == NG - 1):
                         e.matmul(o, l, r, start=st, stop=sp),
                         reads=[r_xsb, rwq], writes=[rbk], inc=(ig == NG - 1),
                         tag="at_mm_v")
                p.op("dve", lambda e, o=a1v(cx, VT + (2 + tt) * 1024 + 512 * h2, 512),
                     i=bk[:, 0:512]: e.tensor_copy(o, i),
                     reads=[rbk], writes=[r_vt], tag="at_evv")

        npieces = (L + 511) // 512
        for g in range(NG):
            j_start = max(t0 - 256, 0)
            kcs = list(range(j_start, t0 + L - 128 + 1, 128))
            last_touch = {}
            for ji, j in enumerate(kcs):
                a = max(j, t0)
                b_ = min(j + 384, t0 + L)
                if b_ <= a:
                    continue
                for h01 in range(2):
                    for pc in range(npieces):
                        lo = max(a - t0, pc * 512)
                        hi = min(b_ - t0, min(pc * 512 + 512, L))
                        if lo < hi:
                            last_touch[pc] = (ji, h01)
            for pc in range(npieces):
                w = min(512, L - pc * 512)
                for bi in (0 + pc % 2, 2 + pc % 2):
                    bz, rbz = cx.psum[bi], cx.bank[bi]
                    p.op("pe", lambda e, o=bz[:, 0:w], z=cx.zrow_sb[:, 0:w]:
                         e.matmul(o, cx.ones_sb[0:1, :], z,
                                  start=True, stop=False),
                         reads=[region(cx, ("fix", "zrow")),
                                region(cx, ("fix", "ones"))],
                         writes=[rbz], tag="at_zb")
            for ji, j in enumerate(kcs):
                kloc = j - t0 + 256
                a = max(j, t0)
                b_ = min(j + 384, t0 + L)
                width = b_ - a
                if width <= 0:
                    continue
                delta = a - j
                for h01 in range(2):
                    pr = slice(64 * h01, 64 * h01 + 64)
                    bs_i = 4 + 2 * (ji % 2) + h01
                    bs, rbs = cx.psum[bs_i], cx.bank[bs_i]
                    p.op("pe", lambda e, o=bs[:, 0:width],
                         l=a1v(cx, KK + g * (256 + ATC) + kloc, 128)[pr, :],
                         r=a1v(cx, QQ + g * L + (a - t0), width)[pr, :]:
                         e.matmul(o, l, r, start=True, stop=False),
                         reads=[r_kk, r_qq], writes=[rbs], inc=False, tag="at_sim")
                    p.op("pe", lambda e, o=bs[:, 0:width],
                         r=cx.mask_sb[delta][:, 0:width]:
                         e.matmul(o, cx.ident_sb[:, :], r, start=False, stop=True),
                         reads=[region(cx, ("fix", f"mask{delta}"))], writes=[rbs],
                         tag="at_msk")
                    ept = a2v(cx, (EPT0, EPT1)[h01], width)
                    p.op("act", lambda e, o=ept, i=bs[:, 0:width]:
                         e.activation(o, i, AF.Exp, bias=cx.cols[:, 74:75]),
                         reads=[rbs, cx.rc], writes=[r_ept[h01]], tag="at_exp")
                    for pc in range(npieces):
                        p0_, p1_ = pc * 512, min(pc * 512 + 512, L)
                        lo = max(a - t0, p0_)
                        hi = min(b_ - t0, p1_)
                        if lo >= hi:
                            continue
                        ew = slice(lo - (a - t0), hi - (a - t0))
                        br, rbr = cx.psum[0 + pc % 2], cx.bank[0 + pc % 2]
                        bo, rbo = cx.psum[2 + pc % 2], cx.bank[2 + pc % 2]
                        is_last = last_touch.get(pc) == (ji, h01)
                        p.op("pe", lambda e, o=br[64 * h01:64 * h01 + 64, lo - p0_:hi - p0_],
                             l=cx.ones_sb[:, 0:64], r=ept[:, ew], sp=is_last:
                             e.matmul(o, l, r, start=False, stop=sp),
                             reads=[r_ept[h01], region(cx, ("fix", "ones"))],
                             writes=[rbr], tag="at_rs")
                        p.op("pe", lambda e, o=bo[64 * h01:64 * h01 + 64, lo - p0_:hi - p0_],
                             l=a1v(cx, VT + (kloc // 128) * 1024 + (2 * g + h01) * 64, 64),
                             r=ept[:, ew], sp=is_last:
                             e.matmul(o, l, r, start=False, stop=sp),
                             reads=[r_ept[h01], r_vt], writes=[rbo], tag="at_pv")
            for pc in range(npieces):
                p0_, p1_ = pc * 512, min(pc * 512 + 512, L)
                w = p1_ - p0_
                br, rbr = cx.psum[0 + pc % 2], cx.bank[0 + pc % 2]
                bo, rbo = cx.psum[2 + pc % 2], cx.bank[2 + pc % 2]
                rec = a3v(cx, RECIP, w)
                p.op("act", lambda e, o=rec, i=br[:, 0:w]: e.activation(o, i, AF.Ln),
                     reads=[rbr], writes=[r_recip], tag="at_lnr")
                p.op("act", lambda e, o=rec, i=rec: e.activation(o, i, AF.Exp,
                                                                 scale=-1.0),
                     reads=[r_recip], writes=[r_recip], tag="at_exr")
                p.op("dve", lambda e, o=a1v(cx, AO + g * L + p0_, w), a=bo[:, 0:w],
                     b=rec: e.tensor_tensor(o, a, b, ALU.mult),
                     reads=[rbo, r_recip], writes=[r_ao], tag="at_ev_o")

        for c0, w in n_sl(L):
            p.dma("sp", sb3(a3v(cx, XRES, NG * w), w),
                  cm_view(src, sbase + t0 + c0, w), writes=[r_xres],
                  tag="at_ldres")
            for og in range(NG):
                bk, rbk = cx.psum[6 + og % 2], cx.bank[6 + og % 2]
                for ig in range(NG):
                    p.op("pe", lambda e, o=bk[:, 0:w],
                         l=a2v(cx, WOSB + ig * C + og * 128, 128),
                         r=a1v(cx, AO + ig * L + c0, w),
                         st=(ig == 0), sp=(ig == NG - 1):
                         e.matmul(o, l, r, start=st, stop=sp),
                         reads=[r_ao, r_wo], writes=[rbk], inc=(ig == NG - 1),
                         tag="at_mm_wo")
                p.op("dve", lambda e, og=og, o=a3v(cx, XRES + og * w, w),
                     a=bk[:, 0:w]: e.tensor_tensor(o, a, o, ALU.add),
                     reads=[rbk, r_xres], writes=[r_xres], tag="at_res")
            p.dma("sp", cm_view(dst, 1 + t0 + c0, w),
                  sb3(a3v(cx, XRES, NG * w), w),
                  reads=[r_xres], writes=[r_dst], tag="at_stres")

        if DEBUG_TAP in ("qq", "kk", "ao"):
            toff, tres = {"qq": (QQ, r_qq), "kk": (KK, r_kk),
                          "ao": (AO, r_ao)}[DEBUG_TAP]
            stride = (256 + ATC) if DEBUG_TAP == "kk" else L
            skip = 256 if DEBUG_TAP == "kk" else 0
            for c0 in range(0, L, 512):
                w = min(512, L - c0)
                r_tapb = region(cx, ("a3", XRES))
                for g in range(NG):
                    p.op("dve", lambda e, o=a3v(cx, XRES + g * w, w),
                         i=a1v(cx, toff + g * stride + skip + c0, w):
                         e.tensor_copy(o, i),
                         reads=[tres], writes=[r_tapb], tag="tapcp")
                p.dma("sp", cm_view(cx.out_hbm, t0 + c0, w),
                      sb3(a3v(cx, XRES, NG * w), w),
                      reads=[r_tapb], writes=[cx.r_hbm["out"]], tag="tapst")

        if t0 + L < FT:
            for g in range(NG):
                p.op("dve", lambda e, g=g, L=L: e.tensor_copy(
                    a1v(cx, KK + g * (256 + ATC), 256),
                    a1v(cx, KK + g * (256 + ATC) + L, 256)),
                    reads=[r_kk], writes=[r_kk], tag="at_ktail")
            p.op("dve", lambda e, L=L: e.tensor_copy(
                a1v(cx, VT, 2 * 1024), a1v(cx, VT + (L // 128) * 1024, 2 * 1024)),
                reads=[r_vt], writes=[r_vt], tag="at_vtail")


def emit_ffn(cx, layer):
    p = cx.prog
    src, sbase, dst = cx.sa_hbm, 1, cx.sb_hbm
    r_src = cx.r_hbm["sa"]
    r_dst = cx.r_hbm["sb"]
    A1, A2, A3 = cx.A1, cx.A2, cx.A3
    A1.reset(); A2.reset(); A3.reset()

    W1O = A1.take(NG * DI2)
    r_w1 = region(cx, ("a1", W1O))
    for g in range(NG):
        p.dma("pool", a1v(cx, W1O + g * DI2, DI2),
              cx.f_w1[layer, 128 * g:128 * (g + 1), :], writes=[r_w1],
              tag="ff_ldw1")
    W2O = A2.take(NPAIR * C)
    r_w2 = region(cx, ("a2", W2O))
    for icc in range(NPAIR):
        rows = min(128, DI - 128 * icc)
        p.dma("pool", cx.a2[0:rows, W2O + icc * C:W2O + (icc + 1) * C],
              cx.f_w2[layer, 128 * icc:128 * icc + rows, :], writes=[r_w2],
              tag="ff_ldw2")

    _x2f0 = A3.take(NG * FFC)
    _xnb0 = A3.take(NG * FFC // 2)
    X2F = [_x2f0, _x2f0]
    XNB = [_xnb0, _xnb0]
    HB = A3.take(NPAIR * FFC // 2)
    SCR = A3.take(5 * FFC)
    GL = A3.take(FFC // 2)
    r_x2f = [region(cx, ("a3", X2F[0])), region(cx, ("a3", X2F[1]))]
    r_xnb = [region(cx, ("a3", XNB[0])), region(cx, ("a3", XNB[1]))]
    r_h = region(cx, ("a3", HB))
    r_gl = region(cx, ("a3", GL))

    for j in range(NG):
        p.dma("sp", cx.cols[:, 48 + j:49 + j],
              cx.f_lng[layer, j:j + 1, :].rearrange("a p -> p a"),
              writes=[cx.rc], tag="ff_lng")
        p.dma("sp", cx.cols[:, 56 + j:57 + j],
              cx.f_lnb[layer, j:j + 1, :].rearrange("a p -> p a"),
              writes=[cx.rc], tag="ff_lnb")
        p.dma("sp", cx.cols[:, 64 + j:65 + j],
              cx.f_b2[layer, j:j + 1, :].rearrange("a p -> p a"),
              writes=[cx.rc], tag="ff_b2")
    for pr in range(NPAIR):
        wa = min(128, DI - 128 * pr)
        p.dma("sp", cx.cols[0:wa, 96 + pr:97 + pr],
              cx.f_b1[layer:layer + 1, 128 * pr:128 * pr + wa].rearrange("a p -> p a"),
              writes=[cx.rc], tag="ff_b1a")
        p.dma("sp", cx.cols[0:wa, 118 + pr:119 + pr],
              cx.f_b1[layer:layer + 1, DI + 128 * pr:DI + 128 * pr + wa].rearrange(
                  "a p -> p a"),
              writes=[cx.rc], tag="ff_b1g")

    col = lambda j: cx.cols[:, j:j + 1]

    for ci, (t0, L) in enumerate(_chunks(FT, FFC)):
        pp = ci % 2
        x2f, rx2f = X2F[pp], r_x2f[pp]
        xnbo, rxnb = XNB[pp], r_xnb[pp]
        p.dma("sp", sb3(a3v(cx, x2f, 4 * L), L),
              src[0:512, sbase + t0:sbase + t0 + L].rearrange(
                  "(g p) t -> p g t", p=128),
              writes=[rx2f], tag="ff_ldx_a")
        p.dma("sp", sb3(a3v(cx, x2f + 4 * L, 4 * L), L),
              src[512:1024, sbase + t0 - 1:sbase + t0 - 1 + L].rearrange(
                  "(g p) t -> p g t", p=128),
              writes=[rx2f], tag="ff_ldx_s")

        xnb = a3vb(cx, xnbo, NG * L)

        def wr_xnb(g, t2, r_t2, xnb=xnb, L=L, rxnb=rxnb):
            p.op("dve", lambda e, o=xnb[:, g * L:(g + 1) * L], t2=t2,
                 s1=col(48 + g), s2=col(56 + g):
                 e.tensor_scalar(o, t2, s1, s2, ALU.mult, ALU.add),
                 reads=[r_t2, cx.rc], writes=[rxnb], tag="ff_xnb")

        emit_ln(cx, f"ffln{layer}@{t0}", L, rx2f, x2f, wr_xnb, 6, 7, SCR)

        hb = a3vb(cx, HB, NPAIR * FFC)
        for pr in range(NPAIR):
            wa = min(128, DI - 128 * pr)
            ba, rba = cx.psum[0 + pr % 2], cx.bank[0 + pr % 2]
            bg, rbg = cx.psum[2 + pr % 2], cx.bank[2 + pr % 2]
            for ig in range(NG):
                p.op("pe", lambda e, o=ba[0:wa, 0:L],
                     l=a1v(cx, W1O + ig * DI2 + 128 * pr, wa),
                     r=xnb[:, ig * L:ig * L + L],
                     st=(ig == 0), sp=(ig == NG - 1):
                     e.matmul(o, l, r, start=st, stop=sp),
                     reads=[rxnb, r_w1], writes=[rba], inc=(ig == NG - 1),
                     tag="ff_mma")
            for ig in range(NG):
                p.op("pe", lambda e, o=bg[0:wa, 0:L],
                     l=a1v(cx, W1O + ig * DI2 + DI + 128 * pr, wa),
                     r=xnb[:, ig * L:ig * L + L],
                     st=(ig == 0), sp=(ig == NG - 1):
                     e.matmul(o, l, r, start=st, stop=sp),
                     reads=[rxnb, r_w1], writes=[rbg], inc=(ig == NG - 1),
                     tag="ff_mmg")
            gl = a3vb(cx, GL, FFC)[0:wa, 0:L]
            p.op("act", lambda e, o=gl, i=bg[0:wa, 0:L],
                 bb=cx.cols[0:wa, 118 + pr:119 + pr]:
                 e.activation(o, i, AF.Gelu, bias=bb),
                 reads=[rbg, cx.rc], writes=[r_gl], tag="ff_gelu")
            p.op("dve", lambda e, o=hb[0:wa, pr * L:pr * L + L], i=ba[0:wa, 0:L],
                 s=cx.cols[0:wa, 96 + pr:97 + pr], b=gl:
                 e.scalar_tensor_tensor(o, i, s, b, ALU.add, ALU.mult),
                 reads=[rba, r_gl, cx.rc], writes=[r_h], tag="ff_h")

        p.dma("sp", sb3(a3v(cx, x2f, NG * L), L), cm_view(src, sbase + t0, L),
              writes=[rx2f], tag="ff_ldxh")

        XOUT = x2f
        for og in range(NG):
            bk, rbk = cx.psum[4 + og % 2], cx.bank[4 + og % 2]
            for icc in range(NPAIR):
                wa = min(128, DI - 128 * icc)
                p.op("pe", lambda e, o=bk[:, 0:L],
                     l=cx.a2[0:wa, W2O + icc * C + 128 * og:W2O + icc * C + 128 * (og + 1)],
                     r=hb[0:wa, icc * L:icc * L + L],
                     st=(icc == 0), sp=(icc == NPAIR - 1):
                     e.matmul(o, l, r, start=st, stop=sp),
                     reads=[r_h, r_w2], writes=[rbk], inc=(icc == NPAIR - 1),
                     tag="ff_mm2")
            p.op("dve", lambda e, og=og, o=a3v(cx, XOUT + og * L, L),
                 i=bk[:, 0:L], s=col(64 + og), b=a3v(cx, x2f + og * L, L):
                 e.scalar_tensor_tensor(o, i, s, b, ALU.add, ALU.add),
                 reads=[rbk, rx2f, cx.rc], writes=[rx2f], tag="ff_res")
        p.dma("sp", cm_view(dst, 1 + t0, L), sb3(a3v(cx, XOUT, NG * L), L),
              reads=[rx2f], writes=[r_dst], tag="ff_st")


def emit_combine(cx):
    if OUT_MODE == "tap":
        return
    p = cx.prog
    A = cx.A3
    A.reset()
    ST = A.take(NG * CMC)
    SS = A.take(NG * CMC)
    XX = A.take(NG * CMC)
    GG = A.take(NG * CMC)
    r_st = region(cx, ("a3", "cmst"))
    r_ss = region(cx, ("a3", "cmss"))
    r_xx = region(cx, ("a3", "cmxx"))
    r_gg = region(cx, ("a3", "cmgg"))
    stream = cx.sb_hbm if N_LAYERS > 0 else cx.xn_hbm
    stream_base = 1 if N_LAYERS > 0 else 1 + WARM
    r_stream = hbm_res_of(cx, stream)
    for (t0, L) in _chunks(FT, CMC):
        if OUT_MODE == "xn":
            p.dma("sp", sb3(a3v(cx, GG, NG * L), L), cm_view(cx.xn_hbm, 1 + WARM + t0, L),
                  extra=[cx.r_hbm["xn"].w], writes=[r_gg], tag="cm_ldxn")
        elif OUT_MODE == "s":
            p.dma("sp", sb3(a3v(cx, GG, NG * L), L), cm_view(cx.s_hbm, t0, L),
                  extra=[cx.r_hbm["s"].w], writes=[r_gg], tag="cm_lds")
        elif OUT_MODE == "stream":
            p.dma("sp", sb3(a3v(cx, GG, NG * L), L), cm_view(stream, stream_base + t0, L),
                  extra=[r_stream.w], writes=[r_gg], tag="cm_ldst")
        else:
            p.dma("sp", sb3(a3v(cx, ST, NG * L), L), cm_view(stream, stream_base + t0, L),
                  extra=[r_stream.w], writes=[r_st], tag="cm_ldst")
            p.dma("sp", sb3(a3v(cx, SS, NG * L), L), cm_view(cx.s_hbm, t0, L),
                  extra=[cx.r_hbm["s"].w], writes=[r_ss], tag="cm_lds")
            p.dma("sp", sb3(a3v(cx, XX, NG * L), L), cm_view(cx.x_ext, WARM + t0, L),
                  writes=[r_xx], tag="cm_ldx")
            p.op("act", lambda e, L=L: e.activation(a3v(cx, GG, NG * L),
                                                    a3v(cx, ST, NG * L), AF.Silu),
                 reads=[r_st], writes=[r_gg], tag="cm_silu")
            p.op("dve", lambda e, L=L: e.tensor_tensor(
                a3v(cx, GG, NG * L), a3v(cx, GG, NG * L), a3v(cx, SS, NG * L),
                ALU.mult), reads=[r_gg, r_ss], writes=[r_gg], tag="cm_mul")
            p.op("dve", lambda e, L=L: e.tensor_tensor(
                a3v(cx, GG, NG * L), a3v(cx, GG, NG * L), a3v(cx, XX, NG * L),
                ALU.add), reads=[r_gg, r_xx], writes=[r_gg], tag="cm_add")
        p.dma("sp", cm_view(cx.out_hbm, t0, L), sb3(a3v(cx, GG, NG * L), L),
              reads=[r_gg], writes=[cx.r_hbm["out"]], tag="cm_st")


def emit_fence(cx):
    p = cx.prog
    p.raw_wait("sp", [(s, s.count) for s in p.qpool["sp"] if s.count > 0],
               tag="fence_sp")
    p.raw_wait("pool", [(s, s.count) for s in p.qpool["pool"] if s.count > 0],
               tag="fence_pool")


_CACHE = {}


def _build():
    key = (N_LAYERS, OUT_MODE, RUN_PHASES, DEBUG_TAP)
    if key in _CACHE:
        return _CACHE[key]
    cx = Ctx()
    cx.nc = bass.Bass("TRN2", target_bir_lowering=False)
    cx.prog = Prog()
    declare_dram(cx)
    with ExitStack() as stack:
        alloc_sbuf(cx, stack)
        emit_setup(cx)
        if "p0" in RUN_PHASES:
            cx.prog.fence()
            emit_p0(cx)
        if "rwkv" in RUN_PHASES:
            cx.prog.fence()
            emit_rwkv(cx)
        if "layers" in RUN_PHASES:
            for L in range(N_LAYERS):
                cx.prog.fence()
                emit_attn(cx, L)
                cx.prog.fence()
                emit_ffn(cx, L)
        cx.prog.fence()
        emit_combine(cx)
        emit_fence(cx)
        with cx.nc.Block() as block:
            @block.tensor
            def _(t):
                cx.prog.replay("pe", t)

            @block.scalar
            def _(s):
                cx.prog.replay("act", s)

            @block.vector
            def _(v):
                cx.prog.replay("dve", v)

            @block.gpsimd
            def _(g):
                cx.prog.replay("pool", g)

            @block.sync
            def _(sy):
                cx.prog.replay("sp", sy)
    _CACHE[key] = cx
    return cx


def kernel(**inputs):
    in_maps = host_prepare(inputs)
    cx = _build()
    res = run_bass_kernel_spmd(cx.nc, in_maps, core_ids=list(range(8)))
    return host_assemble(res.results)
